# version 10
# baseline (speedup 1.0000x reference)
import contextlib
import numpy as np
import concourse.bass as bass
import concourse.mybir as mybir
from concourse.bass_utils import run_bass_kernel_spmd

F32 = mybir.dt.float32
AF = mybir.ActivationFunctionType
ALU = mybir.AluOpType
AX = mybir.AxisListType

S, C, NSEQ = 2048, 1024, 2
NH, DH = 16, 64
NE, HID = 8, 256
TT, TG = 128, 512
NT, NG, NCH = S // TT, S // TG, C // 128
LN_EPS = 1e-5
ARENA_N = 51392
O_XN, O_CTX, O_X1, O_AUX = 0, 16384, 32768, 49152

ENGS = ("tensor", "vector", "scalar", "syncq", "gpq")
DMA_Q = ("syncq", "gpq")

DEBUG_TAPS = ()
PTAP = 2
PARANOID = True
NPAIRS = 8
PHASES = 'ABCDEFG'
NSEQ_DEV = 2
SKIP_HEADS = False
SKIP_NORM = False
SKIP_CTX = False
BARRIER = False


class Rgn:
    __slots__ = ("name", "w", "rs", "pw")

    def __init__(self, name):
        self.name = name
        self.w = None
        self.rs = []
        self.pw = []


class _Op:
    __slots__ = ("fn", "waits", "needs_inc", "dma")


NSEM_POOL = 16


class Sched:

    def __init__(self):
        self.ops = {e: [] for e in ENGS}
        self.dmas = {e: [] for e in DMA_Q}
        self.ndma = {e: 0 for e in DMA_Q}

    def op(self, eng, fn, reads=(), writes=(), pwrites=(), force_inc=False):
        o = _Op()
        o.fn = fn
        o.needs_inc = force_inc or (eng in DMA_Q)
        o.waits = []
        idx = len(self.ops[eng])
        tok = (eng, idx)
        if eng in DMA_Q:
            n = self.ndma[eng]
            o.dma = (n % NSEM_POOL, n // NSEM_POOL + 1)
            self.ndma[eng] = n + 1
        else:
            o.dma = None
        edges = []
        for r in reads:
            if r.w is not None:
                edges.append((r.w, True))
            for t in r.pw:
                edges.append((t, True))
        for w_ in writes:
            if w_.w is not None:
                edges.append((w_.w, False))
            for t in w_.pw:
                edges.append((t, False))
            for t in w_.rs:
                edges.append((t, False))
        for w_ in pwrites:
            if w_.w is not None:
                edges.append((w_.w, False))
            for t in w_.rs:
                edges.append((t, False))
        for (we, wi), is_raw in edges:
            if we == eng == "tensor" and not is_raw:
                continue
            if (not PARANOID) and we == eng and eng not in DMA_Q and not is_raw \
                    and eng != "scalar":
                continue
            if (we, wi) == tok:
                continue
            o.waits.append((we, wi))
            self.ops[we][wi].needs_inc = True
        self.ops[eng].append(o)
        for r in reads:
            r.rs.append(tok)
        for w_ in writes:
            w_.w = tok
            w_.rs = []
            w_.pw = []
        for w_ in pwrites:
            w_.pw.append(tok)
        return tok

    def barrier(self):
        toks = []
        for e in ENGS:
            for i in range(len(self.ops[e]) - 1, -1, -1):
                o = self.ops[e][i]
                if (e in DMA_Q and o.dma is not None) or (e not in DMA_Q and o.fn is not None):
                    toks.append((e, i))
                    break
        for e in ENGS:
            other = [t for t in toks if t[0] != e]
            if other:
                self.fence(e, other)

    def fence(self, eng, tokens):
        o = _Op()
        o.fn = None
        o.needs_inc = False
        o.dma = None
        o.waits = list(tokens)
        for (we, wi) in tokens:
            self.ops[we][wi].needs_inc = True
        self.ops[eng].append(o)

    def finalize(self, sems):
        incval = {}
        for e in ENGS:
            if e in DMA_Q:
                continue
            v, vals = 0, []
            for o in self.ops[e]:
                if o.needs_inc:
                    v += 1
                vals.append(v)
            incval[e] = vals

        def resolve(we, wi):
            if we in DMA_Q:
                slot, cnt = self.ops[we][wi].dma
                return (f"{we}{slot}", 16 * cnt)
            return (we, incval[we][wi])

        hw = {e: {} for e in ENGS}

        def run_engine(e, eng_obj):
            my_hw = hw[e]
            for o in self.ops[e]:
                best = {}
                for (we, wi) in o.waits:
                    sname, v = resolve(we, wi)
                    assert v > 0, f"wait on op without inc: {we}[{wi}]"
                    if v > best.get(sname, 0):
                        best[sname] = v
                if o.dma is not None:
                    slot, cnt = o.dma
                    if cnt > 1:
                        sname = f"{e}{slot}"
                        if best.get(sname, 0) < 16 * (cnt - 1):
                            best[sname] = 16 * (cnt - 1)
                for sname in sorted(best):
                    v = best[sname]
                    if my_hw.get(sname, 0) >= v:
                        continue
                    eng_obj.wait_ge(sems[sname], v)
                    my_hw[sname] = v
                if o.fn is not None:
                    ins = o.fn(eng_obj)
                    if o.dma is not None:
                        slot, cnt = o.dma
                        ins.then_inc(sems[f"{e}{slot}"], 16)
                    elif o.needs_inc:
                        ins.then_inc(sems[e], 1)
        return run_engine


class Arena:

    def __init__(self, handle):
        self.h = handle
        self.live = []

    def claim(self, off, size, name):
        end = off + size
        rgn = Rgn(name)
        keep = []
        for (o0, e0, r) in self.live:
            if e0 <= off or o0 >= end:
                keep.append((o0, e0, r))
            else:
                if r.w is not None:
                    rgn.rs.append(r.w)
                rgn.rs.extend(r.rs)
                rgn.rs.extend(r.pw)
                if o0 < off:
                    keep.append((o0, off, r))
                if e0 > end:
                    keep.append((end, e0, r))
        keep.append((off, end, rgn))
        self.live = keep
        return rgn

    def ap(self, off, size, p0=0, p1=128):
        return self.h[p0:p1, off:off + size]


def ts(i, n):
    return slice(i * n, (i + 1) * n)


def build_program(debug_taps=()):
    nc = bass.Bass("TRN2", target_bir_lowering=False, debug=False,
                   dynamic_dma_scratch_size=4096)

    def din(name, shape):
        return nc.dram_tensor(name, list(shape), F32, kind="ExternalInput").ap()

    def dout(name, shape):
        return nc.dram_tensor(name, list(shape), F32, kind="ExternalOutput").ap()

    x0 = din("x0", (NSEQ, S, C))
    WqT = din("WqT", (C, C)); WkT = din("WkT", (C, C)); WvT = din("WvT", (C, C))
    WoT_d = din("WoT", (C, C))
    b_qkv = din("b_qkv", (128, 24))
    slopes_diag = din("slopes_diag", (128, 8))
    slopes_d1 = din("slopes_d1", (128, 8))
    slopes_s1 = din("slopes_s1", (128, 8))
    WgT_d = din("WgT", (128, 64))
    W1_d = din("W1", (NE, C, HID)); W2_d = din("W2", (NE, HID, C))
    lnsc_c = din("lnsc_c", (128, 64))
    lnb_c = din("lnb_c", (128, 64))
    b1c = din("b1c", (128, 16))
    biasmm0 = din("biasmm0", (9, C))
    qkvT_d = din("qkvT", (C, C)); outwT_d = din("outwT", (C, C))
    biasmm1 = din("biasmm1", (1, C))
    temp16 = din("temp16", (16, 1))
    WrT_d = din("WrT", (C, C)); WkT1_d = din("WkT1", (C, C)); WvT1_d = din("WvT1", (C, C))
    mur_c = din("mur_c", (128, 8)); muk_c = din("muk_c", (128, 8))
    ident_d = din("ident", (128, 128))
    aug2_d = din("aug2", (128, S))
    onesq_d = din("onesq", (128, 128))
    e2_d = din("e2", (128, 2))
    maskblk_d = din("maskblk", (16, C))
    y = dout("y", (NSEQ, S, C))
    scr_a1 = nc.dram_tensor("scr_a1", [S, C], F32, kind="Internal").ap()
    scr_sh = nc.dram_tensor("scr_sh", [64, 2048], F32, kind="Internal").ap()
    scr_r = nc.dram_tensor("scr_r", [S, C], F32, kind="Internal").ap()

    ctx = contextlib.ExitStack()

    def sbt(name, p, n):
        return ctx.enter_context(nc.sbuf_tensor(name, [p, n], F32))

    c_ident = sbt("c_ident", 128, 128)
    c_ones = sbt("c_ones", 128, 128)
    c_rows = sbt("c_rows", 128, 1024)
    c_misc = sbt("c_misc", 128, 256)
    c_moe = sbt("c_moe", 128, 128)
    c_temp = sbt("c_temp", 16, 1)
    st_cols = sbt("st_cols", 128, 64)
    c_eps = sbt("c_eps", 128, 1)
    c_eps8 = sbt("c_eps8", 128, 1)
    st_aff = sbt("st_aff", 128, 64)
    st_rows = sbt("st_rows", 128, 2048)
    sm = sbt("sm", 128, 1024)
    AR = Arena(sbt("ARENA", 128, ARENA_N))

    PS_A = ctx.enter_context(nc.psum_tensor("PS_A", [128, 1024], F32))
    PS_B = ctx.enter_context(nc.psum_tensor("PS_B", [128, 1024], F32))
    PS_C = ctx.enter_context(nc.psum_tensor("PS_C", [128, 2048], F32))

    s = Sched()
    r_const = Rgn("const")
    r_stats = Rgn("stats")
    r_aff = Rgn("aff")
    r_rows = Rgn("rows")
    r_sm = Rgn("sm")
    r_psa = Rgn("PS_A"); r_psb = Rgn("PS_B"); r_psc = Rgn("PS_C")
    r_y = Rgn("y"); r_scr = Rgn("scr")

    def dma(q, out, in_, reads=(), writes=(), pwrites=()):
        return s.op("syncq", lambda e, o=out, i=in_: e.dma_start(out=o, in_=i),
                    reads=reads, writes=writes, pwrites=pwrites)

    def vop(fn, reads=(), writes=()):
        return s.op("vector", fn, reads=reads, writes=writes)

    def sop(fn, reads=(), writes=()):
        return s.op("scalar", fn, reads=reads, writes=writes)

    def top(fn, reads=(), writes=(), force_inc=False):
        return s.op("tensor", fn, reads=reads, writes=writes, force_inc=force_inc)

    for out_t, in_ap in [
        (c_ident[:, :], ident_d), (c_ones[:, :], onesq_d),
        (c_rows[0:16, :], maskblk_d), (c_rows[32:41, :], biasmm0), (c_rows[64:65, :], biasmm1),
        (c_misc[:, 0:24], b_qkv), (c_misc[:, 32:40], slopes_diag),
        (c_misc[:, 40:48], slopes_d1), (c_misc[:, 48:56], slopes_s1),
        (c_misc[:, 64:128], WgT_d),
        (c_misc[:, 128:144], b1c), (c_misc[:, 144:152], mur_c), (c_misc[:, 152:160], muk_c),
        (c_misc[:, 184:186], e2_d),
        (c_moe[:, 0:64], lnsc_c), (c_moe[:, 64:128], lnb_c),
        (c_temp[:, :], temp16),
    ]:
        dma("syncq", out_t, in_ap, pwrites=(r_const,))

    SQ8 = float(2.0 ** -1.5)
    vop(lambda e: e.tensor_scalar(out=c_misc[:, 24:32], in0=c_misc[:, 0:8],
        scalar1=SQ8, scalar2=None, op0=ALU.mult), (r_const,), (r_const,))
    vop(lambda e: e.tensor_scalar(out=c_misc[:, 176:184], in0=c_misc[:, 8:16],
        scalar1=SQ8, scalar2=None, op0=ALU.mult), (r_const,), (r_const,))
    vop(lambda e: e.memset(c_eps[:, :], LN_EPS), (), (r_const,))
    vop(lambda e: e.memset(c_eps8[:, :], 1e-8), (), (r_const,))
    vop(lambda e: e.scalar_tensor_tensor(out=c_misc[:, 160:168], in0=c_misc[:, 144:152],
        scalar=-1.0, in1=c_ones[:, 0:8], op0=ALU.mult, op1=ALU.add), (r_const,), (r_const,))
    vop(lambda e: e.scalar_tensor_tensor(out=c_misc[:, 168:176], in0=c_misc[:, 152:160],
        scalar=-1.0, in1=c_ones[:, 0:8], op0=ALU.mult, op1=ALU.add), (r_const,), (r_const,))

    taps = {}

    def mktap(name, shape):
        if name in debug_taps:
            taps[name] = dout("dbg_" + name, shape)
            return taps[name]
        return None

    def fmap(off, cc, t0, t1):
        return AR.ap(off + cc * S + t0, t1 - t0)

    def phase_A(si):
        r_xn = AR.claim(O_XN, C * NCH * 0 + 16384, "xn0")
        r_xsl = [AR.claim(O_CTX + i * C, C, f"xsl{i}") for i in range(2)]
        r_xt = [AR.claim(O_AUX + i * C, C, f"xt{i}") for i in range(2)]
        cS = st_cols
        for it in range(NT):
            xt = AR.ap(O_AUX + (it % 2) * C, C)
            rx = r_xt[it % 2]
            sl = AR.ap(O_CTX + (it % 2) * C, C)
            rsl = r_xsl[it % 2]
            sq = AR.ap(O_CTX + ((it + 1) % 2) * C, C)
            rsq = r_xsl[(it + 1) % 2]
            dma("syncq", xt, x0[si, ts(it, TT), :], writes=(rx,))
            vop(lambda e, xt=xt: e.reduce_sum(cS[:, 0:1], xt, axis=AX.X), (rx,), (r_stats,))
            sop(lambda e, xt=xt, sq=sq: e.activation(out=sq, in_=xt, func=AF.Square,
                accum_out=cS[:, 1:2]), (rx,), (rsq, r_stats))
            vop(lambda e: e.tensor_scalar(out=cS[:, 2:3], in0=cS[:, 0:1],
                scalar1=1.0 / C, scalar2=None, op0=ALU.mult), (r_stats,), (r_stats,))
            vop(lambda e: e.tensor_scalar(out=cS[:, 3:4], in0=cS[:, 1:2],
                scalar1=1.0 / C, scalar2=None, op0=ALU.mult), (r_stats,), (r_stats,))
            vop(lambda e: e.scalar_tensor_tensor(out=cS[:, 4:5], in0=cS[:, 2:3],
                scalar=cS[:, 2:3], in1=cS[:, 3:4], op0=ALU.mult, op1=ALU.subtract),
                (r_stats,), (r_stats,))
            vop(lambda e: e.scalar_tensor_tensor(out=cS[:, 16:17], in0=cS[:, 4:5],
                scalar=-1.0, in1=c_eps[:, 0:1], op0=ALU.mult, op1=ALU.add), (r_stats, r_const), (r_stats,))
            sop(lambda e: e.activation(out=cS[:, 5:6], in_=cS[:, 16:17], func=AF.Sqrt),
                (r_stats,), (r_stats,))
            vop(lambda e: e.reciprocal(out=cS[:, 6:7], in_=cS[:, 5:6]), (r_stats,), (r_stats,))
            vop(lambda e: e.scalar_tensor_tensor(out=cS[:, 7:8], in0=cS[:, 2:3],
                scalar=-1.0, in1=cS[:, 6:7], op0=ALU.mult, op1=ALU.mult),
                (r_stats,), (r_stats,))
            sop(lambda e, xt=xt, sl=sl: e.activation(out=sl, in_=xt, func=AF.Identity,
                scale=cS[:, 6:7], bias=cS[:, 7:8]), (rx, r_stats), (rsl,))
            if it == 0:
                t1 = mktap(f"xntm0_{si}", (128, C))
                if t1 is not None:
                    dma("gpq", t1, sl, reads=(rsl,))
                t2 = mktap(f"stats0_{si}", (128, 20))
                if t2 is not None:
                    dma("gpq", t2, cS[:, 0:20], reads=(r_stats,))
            vop(lambda e, sl=sl: e.reduce_sum(cS[:, 8:9], sl, axis=AX.X), (rsl,), (r_stats,))
            sop(lambda e, sl=sl, sq=sq: e.activation(out=sq, in_=sl, func=AF.Square,
                accum_out=cS[:, 9:10]), (rsl,), (rsq, r_stats))
            vop(lambda e: e.tensor_scalar(out=cS[:, 10:11], in0=cS[:, 8:9],
                scalar1=1.0 / C, scalar2=None, op0=ALU.mult), (r_stats,), (r_stats,))
            vop(lambda e: e.tensor_scalar(out=cS[:, 11:12], in0=cS[:, 9:10],
                scalar1=1.0 / C, scalar2=None, op0=ALU.mult), (r_stats,), (r_stats,))
            vop(lambda e: e.scalar_tensor_tensor(out=cS[:, 12:13], in0=cS[:, 10:11],
                scalar=cS[:, 10:11], in1=cS[:, 11:12], op0=ALU.mult, op1=ALU.subtract),
                (r_stats,), (r_stats,))
            vop(lambda e: e.scalar_tensor_tensor(out=cS[:, 17:18], in0=cS[:, 12:13],
                scalar=-1.0, in1=c_eps[:, 0:1], op0=ALU.mult, op1=ALU.add), (r_stats, r_const), (r_stats,))
            sop(lambda e: e.activation(out=cS[:, 13:14], in_=cS[:, 17:18], func=AF.Sqrt),
                (r_stats,), (r_stats,))
            vop(lambda e: e.reciprocal(out=cS[:, 14:15], in_=cS[:, 13:14]), (r_stats,), (r_stats,))
            vop(lambda e, it=it: e.scalar_tensor_tensor(out=st_aff[:, 4 * it:4 * it + 1],
                in0=cS[:, 6:7], scalar=1.0, in1=cS[:, 14:15], op0=ALU.bypass, op1=ALU.mult),
                (r_stats,), (r_aff,))
            vop(lambda e: e.tensor_sub(cS[:, 15:16], cS[:, 7:8], cS[:, 10:11]),
                (r_stats,), (r_stats,))
            vop(lambda e, it=it: e.scalar_tensor_tensor(out=st_aff[:, 4 * it + 1:4 * it + 2],
                in0=cS[:, 15:16], scalar=1.0, in1=cS[:, 14:15], op0=ALU.bypass, op1=ALU.mult),
                (r_stats,), (r_aff,))
            for cc in range(NCH):
                pst = (PS_A if cc < 4 else PS_B)[:, ts(cc % 4, 128)]
                rps = r_psa if cc < 4 else r_psb
                top(lambda e, sl=sl, cc=cc, pst=pst: e.transpose(
                    pst, sl[:, ts(cc, 128)], c_ident[:, :]), (rsl, r_const), (rps,))
            sop(lambda e, it=it: e.activation(
                out=AR.ap(O_XN, 8 * S).rearrange("p (c t) -> p c t", c=NCH)[:, 0:4, ts(it, TT)],
                in_=PS_A[:, 0:512].rearrange("p (c t) -> p c t", c=4), func=AF.Copy),
                (r_psa,), (r_xn,))
            sop(lambda e, it=it: e.activation(
                out=AR.ap(O_XN, 8 * S).rearrange("p (c t) -> p c t", c=NCH)[:, 4:8, ts(it, TT)],
                in_=PS_B[:, 0:512].rearrange("p (c t) -> p c t", c=4), func=AF.Copy),
                (r_psb,), (r_xn,))
        tap = mktap(f"xn0_{si}", (C, S))
        if tap is not None:
            for cc in range(NCH):
                dma("gpq", tap[ts(cc, 128), :], fmap(O_XN, cc, 0, S), reads=(r_xn,))
        tap = mktap(f"aff_{si}", (128, 64))
        if tap is not None:
            dma("gpq", tap, st_aff[:, :], reads=(r_aff,))

        for g in range(NG):
            for cc in range(NCH):
                top(lambda e, g=g, cc=cc: e.matmul(
                    PS_C[0:8, ts(g, TG)], c_misc[:, 64 + cc * 8: 72 + cc * 8],
                    fmap(O_XN, cc, g * TG, (g + 1) * TG),
                    start=(cc == 0), stop=(cc == NCH - 1)),
                    (r_xn, r_const), (r_psc,))
        sop(lambda e: e.activation(out=st_rows[0:8, :], in_=PS_C[0:8, :], func=AF.Copy),
            (r_psc,), (r_rows,))
        for it in range(NT):
            pst = (PS_A if it % 8 < 4 else PS_B)[:, ts(it % 4, 16)]
            rps = r_psa if it % 8 < 4 else r_psb
            top(lambda e, it=it, pst=pst: e.transpose(
                pst[:, 0:8], st_rows[0:8, ts(it, TT)], c_ident[0:8, 0:8]),
                (r_rows, r_const), (rps,))
            sop(lambda e, it=it, pst=pst: e.activation(out=sm[:, ts(it, 8)], in_=pst[:, 0:8],
                func=AF.Copy), (rps,), (r_sm,))
            vop(lambda e, it=it: e.reduce_max(sm[:, 128 + it:129 + it], sm[:, ts(it, 8)],
                axis=AX.X), (r_sm,), (r_sm,))
            vop(lambda e, it=it: e.tensor_scalar(out=sm[:, 256 + it * 8: 264 + it * 8],
                in0=sm[:, ts(it, 8)], scalar1=sm[:, 128 + it:129 + it], scalar2=None,
                op0=ALU.is_equal), (r_sm,), (r_sm,))
            pst2 = (PS_A if it % 8 < 4 else PS_B)[:, 512 + (it % 4) * 128: 640 + (it % 4) * 128]
            top(lambda e, it=it, pst2=pst2: e.transpose(
                pst2[0:8, :], sm[:, 256 + it * 8: 264 + it * 8], c_ident[:, :]),
                (r_sm, r_const), (rps,))
            sop(lambda e, it=it, pst2=pst2: e.activation(out=st_rows[32:40, ts(it, TT)],
                in_=pst2[0:8, :], func=AF.Copy), (rps,), (r_rows,))
        dma("syncq", st_rows[40:41, :], aug2_d[65:66, :], writes=(r_rows,))
        tap = mktap(f"mask_{si}", (9, S))
        if tap is not None:
            dma("gpq", tap, st_rows[32:41, :], reads=(r_rows,))
        return r_xn


    B_QA, B_QB, B_KA, B_KB = O_X1, O_X1 + 2048, O_X1 + 4096, O_X1 + 6144
    B_SX, B_VT, B_PT, B_WS, B_SP = (O_X1 + 8192, O_X1 + 8704, O_X1 + 13856,
                                    O_X1 + 10784, O_X1 + 15904)
    SQ8 = float(2.0 ** -1.5)

    def phase_B(si, r_xn):
        r_aug = AR.claim(O_AUX, 2048, "aug2sb")
        r_qA = AR.claim(B_QA, 2048, "qA"); r_qB = AR.claim(B_QB, 2048, "qB")
        r_kA = AR.claim(B_KA, 2048, "kA"); r_kB = AR.claim(B_KB, 2048, "kB")
        r_sx = AR.claim(B_SX, 512, "slabX")
        r_vt = AR.claim(B_VT, 2080, "vT")
        r_pts = [AR.claim(B_PT + i_ * 512, 512, f"PT{i_}") for i_ in range(4)]
        r_ws = AR.claim(B_WS, 3072, "WS")
        r_sp = AR.claim(B_SP, 480, "SP")
        r_ctx = AR.claim(O_CTX, 16384, "ctx")
        aug_sb = AR.ap(O_AUX, 2048)
        dma("syncq", aug_sb[64:66, :], aug2_d[64:66, :], writes=(r_aug,))
        dma("syncq", AR.ap(B_QA, 2048, 64, 65), aug2_d[65:66, :], writes=(r_qA,))
        dma("syncq", AR.ap(B_QB, 2048, 64, 65), aug2_d[65:66, :], writes=(r_qB,))
        vt3 = AR.ap(B_VT, 2080).rearrange("p (t u) -> p t u", u=130)
        vop(lambda e: e.memset(vt3[:, :, 64:65], 1.0), (), (r_vt,))
        vop(lambda e: e.memset(vt3[:, :, 129:130], 1.0), (), (r_vt,))
        cell = lambda i: AR.ap(B_SP + i, 1, 64, 66)
        vop(lambda e: e.tensor_copy(AR.ap(B_SP + 0, 1, 64, 65), c_ones[64:65, 0:1]),
            (r_const,), (r_sp,))
        vop(lambda e: e.tensor_copy(AR.ap(B_SP + 1, 1, 64, 65), c_ones[64:65, 0:1]),
            (r_const,), (r_sp,))

        wsq = AR.ap(B_WS, 1024); wsk = AR.ap(B_WS + 1024, 1024); wsv = AR.ap(B_WS + 2048, 1024)

        def load_ws(p):
            for wsi, W_d in enumerate((WqT, WkT, WvT)):
                for cc in range(NCH):
                    dma("syncq", AR.ap(B_WS + wsi * 1024 + cc * 128, 128),
                        W_d[ts(cc, 128), ts(p, 128)], reads=(), pwrites=(r_ws,))

        load_ws(0)
        for p in range(NPAIRS):
            for wi, (ws_ap, bcol, dstA, r_dA, r_dB, dstB_off) in enumerate((
                    (wsq, c_misc[:, 24 + p:25 + p], B_QA, r_qA, r_qB, B_QB),
                    (wsk, c_misc[:, 176 + p:177 + p], B_KA, r_kA, r_kB, B_KB),
                    (wsv, c_misc[:, 16 + p:17 + p], None, None, None, None))):
                scale = SQ8 if wi < 2 else 1.0
                for g in range(NG):
                    psw = (PS_A if g % 2 == 0 else PS_B)[:, 0:512]
                    rps = r_psa if g % 2 == 0 else r_psb
                    for cc in range(NCH):
                        top(lambda e, ws_ap=ws_ap, cc=cc, g=g, psw=psw: e.matmul(
                            psw, ws_ap[:, ts(cc, 128)], fmap(O_XN, cc, g * TG, (g + 1) * TG),
                            start=(cc == 0), stop=(cc == NCH - 1)),
                            (r_ws, r_xn), (rps,))
                    if wi < 2:
                        sop(lambda e, psw=psw, g=g, dstA=dstA, bcol=bcol, scale=scale:
                            e.activation(out=AR.ap(dstA + g * TG, TG, 0, 64),
                                         in_=psw[0:64, :], func=AF.Identity,
                                         scale=scale, bias=bcol[0:64, :]),
                            (rps, r_const), (r_dA,))
                        sop(lambda e, psw=psw, bcol=bcol, scale=scale:
                            e.activation(out=AR.ap(B_SX, 512, 64, 128),
                                         in_=psw[64:128, :], func=AF.Identity,
                                         scale=scale, bias=bcol[64:128, :]),
                            (rps, r_const), (r_sx,))
                        dma("gpq", AR.ap(dstB_off + g * TG, TG, 0, 64),
                            AR.ap(B_SX, 512, 64, 128), reads=(r_sx,), writes=(r_dB,))
                        sop(lambda e, psw=psw, bcol=bcol, scale=scale:
                            e.activation(out=AR.ap(B_PT, 512), in_=psw,
                                         func=AF.Square, scale=scale, bias=bcol),
                            (rps, r_const), (r_pts[0],))
                        top(lambda e, g=g: e.matmul(
                            PS_C[64:66, ts(g, TG)], c_misc[:, 184:186], AR.ap(B_PT, 512),
                            start=True, stop=True, tile_position=(0, 64)),
                            (r_pts[0], r_const), (r_psc,))
                    else:
                        sop(lambda e, psw=psw, bcol=bcol: e.activation(
                            out=AR.ap(B_SX, 512), in_=psw, func=AF.Identity,
                            scale=1.0, bias=bcol), (rps, r_const), (r_sx,))
                        if p == PTAP and g == 0:
                            t7 = mktap(f"bsx_{si}", (128, 512))
                            if t7 is not None:
                                dma("gpq", t7, AR.ap(B_SX, 512), reads=(r_sx,))
                        pvb = PS_B if g % 2 == 0 else PS_A
                        rpv = r_psb if g % 2 == 0 else r_psa
                        if p == PTAP and g == 0:
                            t7s = mktap(f"bstale_{si}", (128, 512))
                            if t7s is not None:
                                sop(lambda e, pvb=pvb: e.activation(out=AR.ap(B_PT + 512, 512),
                                    in_=pvb[:, 512:1024], func=AF.Copy), (rpv,), (r_pts[1],))
                                dma("gpq", t7s, AR.ap(B_PT + 512, 512), reads=(r_pts[1],))
                            t7i = mktap(f"bid_{si}", (128, 128))
                            if t7i is not None:
                                dma("gpq", t7i, c_ident[:, :], reads=(r_const,))
                        for tc in range(4):
                            top(lambda e, tc=tc, pvb=pvb: e.transpose(
                                pvb[:, 512 + tc * 128:640 + tc * 128],
                                AR.ap(B_SX + tc * 128, 128), c_ident[:, :]),
                                (r_sx, r_const), (rpv,))
                        sop(lambda e, g=g, pvb=pvb: e.activation(
                            out=AR.ap(B_VT + g * 520, 520).rearrange(
                                "p (t u v) -> p t u v", t=4, v=65)[:, :, :, 0:64],
                            in_=pvb[:, 512:1024].rearrange(
                                "p (t u v) -> p t u v", t=4, v=64),
                            func=AF.Copy), (rpv,), (r_vt,))
                        if p == PTAP and g == 0:
                            t7b = mktap(f"bpv_{si}", (128, 512))
                            if t7b is not None:
                                sop(lambda e, pvb=pvb: e.activation(out=AR.ap(B_PT, 512),
                                    in_=pvb[:, 512:1024], func=AF.Copy), (rpv,), (r_pts[0],))
                                dma("gpq", t7b, AR.ap(B_PT, 512), reads=(r_pts[0],))
                            t7c = mktap(f"bvt2_{si}", (128, 520))
                            if t7c is not None:
                                dma("gpq", t7c, AR.ap(B_VT, 520), reads=(r_vt,))
                if wi == 0:
                    vop(lambda e: e.reduce_max(cell(2), PS_C[64:66, :], axis=AX.X),
                        (r_psc,), (r_sp,))
                elif wi == 1:
                    vop(lambda e: e.reduce_max(cell(3), PS_C[64:66, :], axis=AX.X),
                        (r_psc,), (r_sp,))
            vop(lambda e: e.tensor_add(cell(4), cell(2), cell(3)), (r_sp,), (r_sp,))
            vop(lambda e: e.tensor_scalar(out=cell(5), in0=cell(4), scalar1=-0.5,
                scalar2=None, op0=ALU.mult), (r_sp,), (r_sp,))
            vop(lambda e, p=p: e.scalar_tensor_tensor(out=cell(6),
                in0=c_misc[64:66, 32 + p:33 + p], scalar=-2047.0, in1=cell(5),
                op0=ALU.mult, op1=ALU.add), (r_sp, r_const), (r_sp,))
            dma("gpq", AR.ap(B_SP + 0, 1, 65, 66), AR.ap(B_SP + 6, 1, 64, 65),
                reads=(r_sp,), writes=(r_sp,))
            dma("gpq", AR.ap(B_SP + 1, 1, 65, 66), AR.ap(B_SP + 6, 1, 65, 66),
                reads=(r_sp,), writes=(r_sp,))
            dma("syncq", AR.ap(B_QA, 2048, 65, 66), aug2_d[65:66, :], writes=(r_qA,))
            dma("syncq", AR.ap(B_QB, 2048, 65, 66), aug2_d[65:66, :], writes=(r_qB,))
            vop(lambda e: e.tensor_scalar(out=AR.ap(B_QA, 2048, 64, 66),
                in0=AR.ap(B_QA, 2048, 64, 66), scalar1=cell(0), scalar2=None,
                op0=ALU.mult), (r_sp, r_qA), (r_qA,))
            vop(lambda e: e.tensor_scalar(out=AR.ap(B_QB, 2048, 64, 66),
                in0=AR.ap(B_QB, 2048, 64, 66), scalar1=cell(1), scalar2=None,
                op0=ALU.mult), (r_sp, r_qB), (r_qB,))
            vop(lambda e, p=p: e.tensor_scalar(out=AR.ap(B_KA, 2048, 64, 66),
                in0=aug_sb[64:66, :], scalar1=c_misc[64:66, 40 + p:41 + p], scalar2=None,
                op0=ALU.mult), (r_aug, r_const), (r_kA,))
            vop(lambda e, p=p: e.tensor_scalar(out=AR.ap(B_KB, 2048, 64, 66),
                in0=aug_sb[64:66, :], scalar1=c_misc[64:66, 48 + p:49 + p], scalar2=None,
                op0=ALU.mult), (r_aug, r_const), (r_kB,))
            if p == PTAP:
                t1 = mktap(f"baug_{si}", (8, S))
                if t1 is not None:
                    dma("gpq", t1[0:2, :], AR.ap(B_QA, 2048, 64, 66), reads=(r_qA,))
                    dma("gpq", t1[2:4, :], AR.ap(B_KA, 2048, 64, 66), reads=(r_kA,))
                    dma("gpq", t1[4:6, :], AR.ap(B_QB, 2048, 64, 66), reads=(r_qB,))
                    dma("gpq", t1[6:8, :], AR.ap(B_KB, 2048, 64, 66), reads=(r_kB,))
                t2 = mktap(f"bcell_{si}", (2, 8))
                if t2 is not None:
                    dma("gpq", t2, AR.ap(B_SP, 8, 64, 66), reads=(r_sp,))
                t3 = mktap(f"bq_{si}", (66, S))
                if t3 is not None:
                    dma("gpq", t3, AR.ap(B_QA, 2048, 0, 66), reads=(r_qA,))
                t4 = mktap(f"bvt_{si}", (128, 260))
                if t4 is not None:
                    dma("gpq", t4, AR.ap(B_VT, 260), reads=(r_vt,))
            if BARRIER:
                s.barrier()
            if p + 1 < NPAIRS:
                load_ws(p + 1)
            for a in (() if SKIP_HEADS else (0, 1)):
                QO = B_QA if a == 0 else B_QB
                KO = B_KA if a == 0 else B_KB
                r_q = r_qA if a == 0 else r_qB
                r_k = r_kA if a == 0 else r_kB
                def emit_st(tk, gh, QO=QO, KO=KO, r_q=r_q, r_k=r_k):
                    ps = PS_A if gh == 0 else PS_B
                    rps = r_psa if gh == 0 else r_psb
                    for u in range(2):
                        tq0 = gh * 1024 + u * 512
                        top(lambda e, tk=tk, u=u, ps=ps, tq0=tq0, QO=QO, KO=KO:
                            e.matmul(ps[:, ts(u, 512)],
                                     AR.ap(KO + tk * 128, 128, 0, 66),
                                     AR.ap(QO + tq0, 512, 0, 66),
                                     start=True, stop=True),
                            (r_q, r_k), (rps,))
                    for u in range(2):
                        sop(lambda e, ps=ps, gh=gh, u=u: e.activation(
                            out=AR.ap(B_PT + gh * 1024 + u * 512, 512),
                            in_=ps[:, ts(u, 512)],
                            func=AF.Exp), (rps,), (r_pts[gh * 2 + u],))

                def emit_ctx(tk, gh, a=a):
                    for u in range(2):
                        g = gh * 2 + u
                        top(lambda e, tk=tk, a=a, g=g, gh=gh, u=u: e.matmul(
                            PS_C[0:65, ts(g, 512)],
                            AR.ap(B_VT + tk * 130 + 65 * a, 65),
                            AR.ap(B_PT + gh * 1024 + u * 512, 512),
                            start=(tk == 0), stop=(tk == NT - 1)),
                            (r_vt, r_pts[gh * 2 + u]), (r_psc,))

                grans = [(tk, gh) for tk in range(NT) for gh in range(2)]
                for gi, (tk, gh) in enumerate(grans):
                    emit_st(tk, gh)
                    if not SKIP_CTX and gi >= 1:
                        emit_ctx(*grans[gi - 1])
                if not SKIP_CTX:
                    emit_ctx(*grans[-1])
                if SKIP_NORM or SKIP_CTX:
                    continue
                vop(lambda e: e.reciprocal(out=AR.ap(B_PT, 2048, 64, 65),
                    in_=PS_C[64:65, 0:2048]), (r_psc,), tuple(r_pts))
                if p == PTAP and a == 0:
                    t6 = mktap(f"brcp_{si}", (1, S))
                    if t6 is not None:
                        dma("gpq", t6, AR.ap(B_PT, 2048, 64, 65), reads=tuple(r_pts))
                for g in range(NG):
                    psb_ = (PS_A if g % 2 == 0 else PS_B)[0:64, 0:512]
                    rpsb = r_psa if g % 2 == 0 else r_psb
                    top(lambda e, g=g, psb_=psb_: e.matmul(
                        psb_, c_ones[64:65, 0:64], AR.ap(B_PT + g * 512, 512, 64, 65),
                        start=True, stop=True), (r_pts[g], r_const), (rpsb,))
                    sop(lambda e, psb_=psb_: e.activation(out=sm[0:64, 0:512],
                        in_=psb_, func=AF.Copy), (rpsb,), (r_sm,))
                    if a == 0:
                        vop(lambda e, p=p, g=g: e.tensor_mul(
                            AR.ap(O_CTX + p * S + g * 512, 512, 0, 64),
                            PS_C[0:64, ts(g, 512)], sm[0:64, 0:512]),
                            (r_psc, r_sm), (r_ctx,))
                    else:
                        vop(lambda e, g=g: e.tensor_mul(
                            AR.ap(B_SX, 512, 0, 64), PS_C[0:64, ts(g, 512)],
                            sm[0:64, 0:512]), (r_psc, r_sm), (r_sx,))
                        dma("gpq", AR.ap(O_CTX + p * S + g * 512, 512, 64, 128),
                            AR.ap(B_SX, 512, 0, 64), reads=(r_sx,), writes=(r_ctx,))
            if BARRIER:
                s.barrier()
        tap = mktap(f"ctx_{si}", (C, S))
        if tap is not None:
            for cc in range(NCH):
                dma("gpq", tap[ts(cc, 128), :], AR.ap(O_CTX + cc * S, S), reads=(r_ctx,))
        return r_ctx

    def phase_C(si, r_ctx):
        r_wo = AR.claim(O_XN, 8192, "WoT")
        r_xt = [AR.claim(O_XN + 8192 + i * C, C, f"cxt{i}") for i in range(2)]
        r_x1 = AR.claim(O_X1, 16384, "x1")
        for cc in range(NCH):
            dma("syncq", AR.ap(O_XN + cc * 1024, 1024), WoT_d[ts(cc, 128), :],
                pwrites=(r_wo,))
        for it in range(NT):
            xt = AR.ap(O_XN + 8192 + (it % 2) * C, C)
            rx = r_xt[it % 2]
            dma("syncq", xt, x0[si, ts(it, TT), :], writes=(rx,))
            pshalf = PS_C[:, (it % 2) * 1024:(it % 2) * 1024 + 1024]
            for og in range(2):
                for cc in range(NCH):
                    top(lambda e, it=it, og=og, cc=cc, pshalf=pshalf: e.matmul(
                        pshalf[:, ts(og, 512)],
                        AR.ap(O_CTX + cc * S + it * 128, 128),
                        AR.ap(O_XN + cc * 1024 + og * 512, 512),
                        start=(cc == 0), stop=(cc == NCH - 1)),
                        (r_ctx, r_wo), (r_psc,))
            vop(lambda e, it=it, xt=xt, pshalf=pshalf: e.tensor_add(
                AR.ap(O_X1 + it * 1024, 1024), pshalf, xt),
                (r_psc, rx), (r_x1,))
        tap = mktap(f"x1c_{si}", (S, C))
        if tap is not None:
            for it in range(NT):
                dma("gpq", tap[ts(it, TT), :], AR.ap(O_X1 + it * 1024, 1024), reads=(r_x1,))
        return r_x1

    def phase_D(si, r_x1):
        r_w1 = AR.claim(O_XN, 8192, "W1s")
        r_w2 = AR.claim(O_XN + 8192, 8192, "W2")
        r_xh = AR.claim(O_CTX, 4096, "xhat")
        r_h = AR.claim(O_CTX + 4096, 4096, "hsb")
        r_sl = AR.claim(O_CTX + 8192, 1024, "dslab")
        r_xt = [AR.claim(O_AUX + i * C, C, f"dxt{i}") for i in range(2)]
        for half in range(2):
            for el in range(4):
                ei = half * 4 + el
                for cc in range(NCH):
                    dma("syncq", AR.ap(O_XN + el * 2048 + cc * 256, 256),
                        W1_d[ei, ts(cc, 128), :], pwrites=(r_w1,))
                for cc2 in range(2):
                    dma("syncq", AR.ap(O_XN + 8192 + el * 2048 + cc2 * 1024, 1024),
                        W2_d[ei, ts(cc2, 128), :], pwrites=(r_w2,))
                for cc in range(NCH):
                    top(lambda e, el=el, cc=cc, ei=ei: e.matmul(
                        PS_B[0:1, 0:256], c_moe[:, 64 + ei * 8 + cc:64 + ei * 8 + cc + 1],
                        AR.ap(O_XN + el * 2048 + cc * 256, 256),
                        start=(cc == 0), stop=(cc == NCH - 1)), (r_w1, r_const), (r_psb,))
                for cc in range(NCH):
                    vop(lambda e, el=el, cc=cc, ei=ei: e.tensor_scalar(
                        out=AR.ap(O_XN + el * 2048 + cc * 256, 256),
                        in0=AR.ap(O_XN + el * 2048 + cc * 256, 256),
                        scalar1=c_moe[:, ei * 8 + cc:ei * 8 + cc + 1], scalar2=None,
                        op0=ALU.mult), (r_w1, r_const), (r_w1,))
                sop(lambda e: e.activation(out=sm[64:65, 0:256], in_=PS_B[0:1, 0:256],
                    func=AF.Copy), (r_psb,), (r_sm,))
                for hc in range(2):
                    top(lambda e, hc=hc: e.transpose(
                        PS_B[:, 256 + hc * 128:384 + hc * 128] if False
                        else PS_B[0:128, 256 + hc:257 + hc],
                        sm[64:65, hc * 128:(hc + 1) * 128], c_ident[64:65, 64:65]),
                        (r_sm, r_const), (r_psb,))
                    sop(lambda e, hc=hc, el=el: e.activation(
                        out=sm[:, 528 + el * 2 + hc:529 + el * 2 + hc],
                        in_=PS_B[0:128, 256 + hc:257 + hc], func=AF.Copy),
                        (r_psb,), (r_sm,))
                vop(lambda e, el=el, ei=ei: e.tensor_add(
                    sm[:, 512 + el * 2:514 + el * 2], sm[:, 528 + el * 2:530 + el * 2],
                    c_misc[:, 128 + ei * 2:130 + ei * 2]), (r_sm, r_const), (r_sm,))
            for g in range(NG):
                for tl in range(4):
                    it = g * 4 + tl
                    xt = AR.ap(O_AUX + (tl % 2) * C, C)
                    rx = r_xt[tl % 2]
                    dma("syncq", xt, x0[si, ts(it, TT), :], writes=(rx,))
                    sop(lambda e, xt=xt, it=it: e.activation(
                        out=AR.ap(O_CTX + 8192, 1024), in_=xt, func=AF.Identity,
                        scale=st_aff[:, 4 * it:4 * it + 1],
                        bias=st_aff[:, 4 * it + 1:4 * it + 2]),
                        (rx, r_aff), (r_sl,))
                    for cc in range(NCH):
                        pst = (PS_A if cc < 4 else PS_B)[:, ts(cc % 4, 128)]
                        rps = r_psa if cc < 4 else r_psb
                        top(lambda e, cc=cc, pst=pst: e.transpose(
                            pst, AR.ap(O_CTX + 8192 + cc * 128, 128), c_ident[:, :]),
                            (r_sl, r_const), (rps,))
                    sop(lambda e, tl=tl: e.activation(
                        out=AR.ap(O_CTX, 4096).rearrange("p (c t) -> p c t", c=NCH)
                        [:, 0:4, tl * 128:(tl + 1) * 128],
                        in_=PS_A[:, 0:512].rearrange("p (c t) -> p c t", c=4),
                        func=AF.Copy), (r_psa,), (r_xh,))
                    sop(lambda e, tl=tl: e.activation(
                        out=AR.ap(O_CTX, 4096).rearrange("p (c t) -> p c t", c=NCH)
                        [:, 4:8, tl * 128:(tl + 1) * 128],
                        in_=PS_B[:, 0:512].rearrange("p (c t) -> p c t", c=4),
                        func=AF.Copy), (r_psb,), (r_xh,))
                if half == 0 and g == 0:
                    t10 = mktap(f"dxh_{si}", (128, 4096))
                    if t10 is not None:
                        dma("gpq", t10, AR.ap(O_CTX, 4096), reads=(r_xh,))
                for el in range(4):
                    ei = half * 4 + el
                    for hc in range(2):
                        psh = PS_A[:, ts(hc, 512)]
                        for cc in range(NCH):
                            top(lambda e, el=el, hc=hc, cc=cc, psh=psh: e.matmul(
                                psh, AR.ap(O_XN + el * 2048 + cc * 256 + hc * 128, 128),
                                AR.ap(O_CTX + cc * 512, 512),
                                start=(cc == 0), stop=(cc == NCH - 1)),
                                (r_w1, r_xh), (r_psa,))
                        sop(lambda e, el=el, hc=hc, psh=psh: e.activation(
                            out=AR.ap(O_CTX + 4096 + el * 1024 + hc * 512, 512),
                            in_=psh, func=AF.Gelu,
                            bias=sm[:, 512 + el * 2 + hc:513 + el * 2 + hc]),
                            (r_psa, r_sm), (r_h,))
                        if half == 0 and el == 0 and g == 0:
                            t8 = mktap(f"dh{hc}_{si}", (128, 512))
                            if t8 is not None:
                                dma("gpq", t8, AR.ap(O_CTX + 4096 + hc * 512, 512), reads=(r_h,))
                            t9 = mktap(f"dbf{hc}_{si}", (128, 1))
                            if t9 is not None:
                                dma("gpq", t9, sm[:, 512 + hc:513 + hc], reads=(r_sm,))
                    dma("gpq", sm[64:65, 0:512], st_rows[32 + ei:33 + ei, ts(g, 512)],
                        reads=(r_rows,), writes=(r_sm,))
                    top(lambda e: e.matmul(PS_B[:, 512:1024], c_ones[64:65, 0:128],
                        sm[64:65, 0:512], start=True, stop=True), (r_sm, r_const), (r_psb,))
                    for hc in range(2):
                        vop(lambda e, el=el, hc=hc: e.tensor_mul(
                            AR.ap(O_CTX + 4096 + el * 1024 + hc * 512, 512),
                            AR.ap(O_CTX + 4096 + el * 1024 + hc * 512, 512),
                            PS_B[:, 512:1024]), (r_h, r_psb), (r_h,))
                    if half == 0 and el == 0 and g == 0:
                        t11 = mktap(f"dhm_{si}", (128, 512))
                        if t11 is not None:
                            dma("gpq", t11, AR.ap(O_CTX + 4096, 512), reads=(r_h,))
                    if half == 0 and el == 1 and g == 0:
                        t13 = mktap(f"dhm1_{si}", (128, 1024))
                        if t13 is not None:
                            dma("gpq", t13, AR.ap(O_CTX + 4096 + 1024, 1024), reads=(r_h,))
                        t14 = mktap(f"dw2e1_{si}", (128, 2048))
                        if t14 is not None:
                            dma("gpq", t14, AR.ap(O_XN + 8192 + 2048, 2048), reads=(r_w2,))
                for tl in range(4):
                    it = g * 4 + tl
                    psx = PS_C[:, (tl % 2) * 1024:(tl % 2) * 1024 + 1024]
                    for og in range(2):
                        for el in range(4):
                            for hc in range(2):
                                last = (half == 0 and el == 3 and hc == 1)
                                top(lambda e, el=el, hc=hc, og=og, tl=tl, psx=psx,
                                    last=last: e.matmul(
                                    psx[:, ts(og, 512)],
                                    AR.ap(O_CTX + 4096 + el * 1024 + hc * 512 + tl * 128, 128),
                                    AR.ap(O_XN + 8192 + el * 2048 + hc * 1024 + og * 512, 512),
                                    start=(el == 0 and hc == 0), stop=last),
                                    (r_h, r_w2), (r_psc,))
                        if half == 1:
                            top(lambda e, og=og, it=it, psx=psx: e.matmul(
                                psx[:, ts(og, 512)], st_rows[32:41, ts(it, TT)],
                                c_rows[32:41, og * 512:(og + 1) * 512],
                                start=False, stop=True), (r_rows, r_const), (r_psc,))
                    if half == 0 and it == 0:
                        t12 = mktap(f"dpsx_{si}", (128, 1024))
                        if t12 is not None:
                            sop(lambda e, psx=psx: e.activation(out=sm[:, 0:1024],
                                in_=psx, func=AF.Copy), (r_psc,), (r_sm,))
                            dma("gpq", t12, sm[:, 0:1024], reads=(r_sm,))
                    vop(lambda e, it=it, psx=psx: e.tensor_add(
                        AR.ap(O_X1 + it * 1024, 1024), AR.ap(O_X1 + it * 1024, 1024),
                        psx), (r_psc, r_x1), (r_x1,))
            if half == 0:
                taph = mktap(f"x1h_{si}", (S, C))
                if taph is not None:
                    for it2 in range(NT):
                        dma("gpq", taph[ts(it2, TT), :], AR.ap(O_X1 + it2 * 1024, 1024),
                            reads=(r_x1,))
        tap = mktap(f"x1d_{si}", (S, C))
        if tap is not None:
            for it in range(NT):
                dma("gpq", tap[ts(it, TT), :], AR.ap(O_X1 + it * 1024, 1024), reads=(r_x1,))
        return r_x1


    def phase_E(si, r_x1):
        r_xn1 = AR.claim(O_XN, 16384, "xn1")
        r_sl = AR.claim(O_CTX, 1024, "esl")
        cS = st_cols
        for it in range(NT):
            xt = AR.ap(O_X1 + it * 1024, 1024)
            sl = AR.ap(O_CTX, 1024)
            vop(lambda e, xt=xt: e.reduce_sum(cS[:, 0:1], xt, axis=AX.X), (r_x1,), (r_stats,))
            sop(lambda e, xt=xt, sl=sl: e.activation(out=sl, in_=xt, func=AF.Square,
                accum_out=cS[:, 1:2]), (r_x1,), (r_sl, r_stats))
            vop(lambda e: e.tensor_scalar(out=cS[:, 2:3], in0=cS[:, 0:1],
                scalar1=1.0 / C, scalar2=None, op0=ALU.mult), (r_stats,), (r_stats,))
            vop(lambda e: e.tensor_scalar(out=cS[:, 3:4], in0=cS[:, 1:2],
                scalar1=1.0 / C, scalar2=None, op0=ALU.mult), (r_stats,), (r_stats,))
            vop(lambda e: e.scalar_tensor_tensor(out=cS[:, 4:5], in0=cS[:, 2:3],
                scalar=cS[:, 2:3], in1=cS[:, 3:4], op0=ALU.mult, op1=ALU.subtract),
                (r_stats,), (r_stats,))
            vop(lambda e: e.scalar_tensor_tensor(out=cS[:, 16:17], in0=cS[:, 4:5],
                scalar=-1.0, in1=c_eps[:, 0:1], op0=ALU.mult, op1=ALU.add),
                (r_stats, r_const), (r_stats,))
            sop(lambda e: e.activation(out=cS[:, 5:6], in_=cS[:, 16:17], func=AF.Sqrt),
                (r_stats,), (r_stats,))
            vop(lambda e: e.reciprocal(out=cS[:, 6:7], in_=cS[:, 5:6]), (r_stats,), (r_stats,))
            vop(lambda e: e.scalar_tensor_tensor(out=cS[:, 7:8], in0=cS[:, 2:3],
                scalar=-1.0, in1=cS[:, 6:7], op0=ALU.mult, op1=ALU.mult),
                (r_stats,), (r_stats,))
            sop(lambda e, xt=xt, sl=sl: e.activation(out=sl, in_=xt, func=AF.Identity,
                scale=cS[:, 6:7], bias=cS[:, 7:8]), (r_x1, r_stats), (r_sl,))
            for cc in range(NCH):
                pst = (PS_A if cc < 4 else PS_B)[:, ts(cc % 4, 128)]
                rps = r_psa if cc < 4 else r_psb
                top(lambda e, sl=sl, cc=cc, pst=pst: e.transpose(
                    pst, sl[:, ts(cc, 128)], c_ident[:, :]), (r_sl, r_const), (rps,))
            sop(lambda e, it=it: e.activation(
                out=AR.ap(O_XN, 8 * S).rearrange("p (c t) -> p c t", c=NCH)[:, 0:4, ts(it, TT)],
                in_=PS_A[:, 0:512].rearrange("p (c t) -> p c t", c=4), func=AF.Copy),
                (r_psa,), (r_xn1,))
            sop(lambda e, it=it: e.activation(
                out=AR.ap(O_XN, 8 * S).rearrange("p (c t) -> p c t", c=NCH)[:, 4:8, ts(it, TT)],
                in_=PS_B[:, 0:512].rearrange("p (c t) -> p c t", c=4), func=AF.Copy),
                (r_psb,), (r_xn1,))
        tap = mktap(f"xn1_{si}", (C, S))
        if tap is not None:
            for cc in range(NCH):
                dma("gpq", tap[ts(cc, 128), :], fmap(O_XN, cc, 0, S), reads=(r_xn1,))
        return r_xn1

    def phase_F(si, r_xn1, r_x1):
        r_w = AR.claim(O_CTX, 16384, "w1attn")
        r_blk = AR.claim(O_AUX, 1024, "fblk")
        r_sq = AR.claim(O_AUX + 1024, 1024, "fsq")
        cS = st_cols
        for oc in range(NCH):
            for cc in range(NCH):
                dma("syncq", AR.ap(O_AUX + cc * 128, 128),
                    qkvT_d[ts(cc, 128), ts(oc, 128)], pwrites=(r_blk,))
            for g in range(NG):
                psw = (PS_A if g % 2 == 0 else PS_B)[:, 0:512]
                rps = r_psa if g % 2 == 0 else r_psb
                for cc in range(NCH):
                    top(lambda e, cc=cc, g=g, psw=psw: e.matmul(
                        psw, AR.ap(O_AUX + cc * 128, 128), fmap(O_XN, cc, g * TG, (g + 1) * TG),
                        start=(cc == 0), stop=(cc == NCH - 1)), (r_blk, r_xn1), (rps,))
                sop(lambda e, oc=oc, g=g, psw=psw: e.activation(
                    out=fmap(O_CTX, oc, g * TG, (g + 1) * TG), in_=psw, func=AF.Copy),
                    (rps,), (r_w,))
                sop(lambda e, g=g, psw=psw: e.activation(
                    out=AR.ap(O_AUX + 1024, 512), in_=psw, func=AF.Square,
                    accum_out=cS[:, 20 + g:21 + g]), (rps,), (r_sq, r_stats))
            vop(lambda e, oc=oc: e.reduce_sum(cS[:, 32 + oc:33 + oc], cS[:, 20:24],
                axis=AX.X), (r_stats,), (r_stats,))
        vop(lambda e: e.tensor_scalar(out=cS[:, 40:48], in0=cS[:, 32:40],
            scalar1=1e-24, scalar2=None, op0=ALU.max), (r_stats,), (r_stats,))
        vop(lambda e: e.reciprocal(out=cS[:, 48:56], in_=cS[:, 40:48]), (r_stats,), (r_stats,))
        vop(lambda e: e.memset(sm[:, 0:128], 0.0), (), (r_sm,))
        for oc in range(NCH):
            vop(lambda e, oc=oc: e.tensor_copy(sm[0:64, oc * 16 + 2 * oc:oc * 16 + 2 * oc + 1],
                cS[0:64, 48 + oc:49 + oc]), (r_stats,), (r_sm,))
            vop(lambda e, oc=oc: e.tensor_copy(sm[64:128, oc * 16 + 2 * oc + 1:oc * 16 + 2 * oc + 2],
                cS[64:128, 48 + oc:49 + oc]), (r_stats,), (r_sm,))
        for g in range(NG):
            for oc in range(NCH):
                sop(lambda e, oc=oc, g=g: e.activation(out=AR.ap(O_AUX + 1024, 512),
                    in_=fmap(O_CTX, oc, g * TG, (g + 1) * TG), func=AF.Square),
                    (r_w,), (r_sq,))
                top(lambda e, oc=oc, g=g: e.matmul(
                    PS_C[0:16, ts(g, TG)], sm[:, oc * 16:oc * 16 + 16], AR.ap(O_AUX + 1024, 512),
                    start=(oc == 0), stop=(oc == NCH - 1)), (r_sq, r_sm), (r_psc,))
        sop(lambda e: e.activation(out=st_rows[0:16, :], in_=PS_C[0:16, :], func=AF.Identity,
            scale=c_temp[:, 0:1]), (r_psc, r_const), (r_rows,))
        for it in range(NT):
            pst = (PS_A if it % 2 == 0 else PS_B)[:, 0:16]
            rps = r_psa if it % 2 == 0 else r_psb
            top(lambda e, it=it, pst=pst: e.transpose(
                pst, st_rows[0:16, ts(it, TT)], c_ident[0:16, 0:16]), (r_rows, r_const), (rps,))
            sop(lambda e, pst=pst: e.activation(out=sm[:, 128:144], in_=pst, func=AF.Copy),
                (rps,), (r_sm,))
            vop(lambda e: e.reduce_max(cS[:, 0:1], sm[:, 128:144], axis=AX.X, negate=True),
                (r_sm,), (r_stats,))
            sop(lambda e: e.activation(out=sm[:, 144:160], in_=sm[:, 128:144], func=AF.Exp,
                bias=cS[:, 0:1], accum_out=cS[:, 1:2]), (r_sm, r_stats), (r_sm, r_stats))
            vop(lambda e: e.reciprocal(out=cS[:, 2:3], in_=cS[:, 1:2]), (r_stats,), (r_stats,))
            vop(lambda e, it=it: e.tensor_scalar(out=sm[:, 256 + it * 16:272 + it * 16],
                in0=sm[:, 144:160], scalar1=cS[:, 2:3], scalar2=None, op0=ALU.mult),
                (r_sm, r_stats), (r_sm,))
            pst2 = (PS_A if it % 2 == 0 else PS_B)[:, 128:256]
            top(lambda e, it=it, pst2=pst2: e.transpose(
                pst2[0:16, :], sm[:, 256 + it * 16:272 + it * 16], c_ident[:, :]),
                (r_sm, r_const), (rps,))
            sop(lambda e, it=it, pst2=pst2: e.activation(out=st_rows[0:16, ts(it, TT)],
                in_=pst2[0:16, :], func=AF.Copy), (rps,), (r_rows,))
        vop(lambda e: e.reduce_sum(cS[0:16, 8:9], st_rows[0:16, :], axis=AX.X),
            (r_rows,), (r_stats,))
        vop(lambda e: e.scalar_tensor_tensor(out=cS[0:16, 9:10], in0=cS[0:16, 8:9],
            scalar=1.0, in1=c_eps8[0:16, 0:1], op0=ALU.bypass, op1=ALU.add),
            (r_stats, r_const), (r_stats,))
        vop(lambda e: e.reciprocal(out=cS[0:16, 10:11], in_=cS[0:16, 9:10]),
            (r_stats,), (r_stats,))
        for it in range(NT):
            sop(lambda e, it=it: e.activation(
                out=AR.ap(O_AUX + 1024, 1024), in_=AR.ap(O_CTX, 8 * S).rearrange(
                    "p (c t) -> p c t", c=NCH)[:, :, ts(it, TT)], func=AF.Square),
                (r_w,), (r_sq,))
            for cc in range(NCH):
                pst = (PS_A if cc < 4 else PS_B)[:, ts(cc % 4, 128)]
                rps = r_psa if cc < 4 else r_psb
                top(lambda e, cc=cc, pst=pst: e.transpose(
                    pst, AR.ap(O_AUX + 1024 + cc * 128, 128), c_ident[:, :]),
                    (r_sq, r_const), (rps,))
            sop(lambda e: e.activation(out=AR.ap(O_AUX, 512), in_=PS_A[:, 0:512],
                func=AF.Copy), (r_psa,), (r_blk,))
            sop(lambda e: e.activation(out=AR.ap(O_AUX + 512, 512), in_=PS_B[:, 0:512],
                func=AF.Copy), (r_psb,), (r_blk,))
            for og in range(2):
                top(lambda e, it=it, og=og: e.matmul(
                    PS_C[0:16, og * 512:(og + 1) * 512],
                    sm[:, 256 + it * 16:272 + it * 16], AR.ap(O_AUX + og * 512, 512),
                    start=(it == 0), stop=(it == NT - 1)), (r_sm, r_blk), (r_psc,))
        sop(lambda e: e.activation(out=sm[0:16, 0:1024], in_=PS_C[0:16, 0:1024],
            func=AF.Identity, scale=cS[0:16, 10:11]), (r_psc, r_stats), (r_sm,))
        vop(lambda e: e.tensor_scalar(out=sm[0:16, 0:1024], in0=sm[0:16, 0:1024],
            scalar1=1.0, scalar2=None, op0=ALU.add), (r_sm,), (r_sm,))
        vop(lambda e: e.reciprocal(out=AR.ap(O_AUX + 1024, 1024, 0, 16), in_=sm[0:16, 0:1024]),
            (r_sm,), (r_sq,))
        vop(lambda e: e.tensor_mul(sm[0:16, 0:1024], c_rows[0:16, :],
            AR.ap(O_AUX + 1024, 1024, 0, 16)), (r_const, r_sq), (r_sm,))
        for oc in range(NCH):
            for g in range(NG):
                psx = (PS_A if g % 2 == 0 else PS_B)[:, 0:512]
                rps = r_psa if g % 2 == 0 else r_psb
                top(lambda e, oc=oc, g=g, psx=psx: e.matmul(
                    psx, sm[0:16, oc * 128:(oc + 1) * 128], st_rows[0:16, ts(g, TG)],
                    start=True, stop=True), (r_sm, r_rows), (rps,))
                vop(lambda e, oc=oc, g=g, psx=psx: e.tensor_mul(
                    fmap(O_CTX, oc, g * TG, (g + 1) * TG),
                    fmap(O_CTX, oc, g * TG, (g + 1) * TG), psx), (r_w, rps), (r_w,))
        r_ow = AR.claim(O_AUX, 2048, "fow")
        for oq in range(4):
            for cc in range(NCH):
                dma("syncq", AR.ap(O_AUX + cc * 256, 256),
                    outwT_d[ts(cc, 128), oq * 256:(oq + 1) * 256], pwrites=(r_ow,))
            for it in range(NT):
                psx = (PS_A if it % 2 == 0 else PS_B)[:, 0:256]
                rps = r_psa if it % 2 == 0 else r_psb
                for cc in range(NCH):
                    top(lambda e, cc=cc, it=it, oq=oq, psx=psx: e.matmul(
                        psx, AR.ap(O_CTX + cc * S + it * 128, 128),
                        AR.ap(O_AUX + cc * 256, 256), start=(cc == 0), stop=False),
                        (r_w, r_ow), (rps,))
                top(lambda e, oq=oq, it=it, psx=psx: e.matmul(
                    psx, c_ones[64:65, 0:128], c_rows[64:65, oq * 256:(oq + 1) * 256],
                    start=False, stop=True), (r_const,), (rps,))
                vop(lambda e, it=it, oq=oq, psx=psx: e.tensor_add(
                    sm[:, 0:256], psx,
                    AR.ap(O_X1 + it * 1024 + oq * 256, 256)), (rps, r_x1), (r_sm,))
                dma("gpq", scr_a1[ts(it, TT), oq * 256:(oq + 1) * 256],
                    sm[:, 0:256], reads=(r_sm,), writes=(r_scr,))
        tap = mktap(f"a1_{si}", (S, C))
        if tap is not None:
            dma("gpq", tap, scr_a1, reads=(r_scr,))

    def phase_G(si, r_xn1):
        r_blkg = AR.claim(O_AUX, 1024, "gblk")
        r_sqg = AR.claim(O_AUX + 1024, 1024, "gsq")
        r_ka = AR.claim(O_X1, 16384, "k_act")
        r_wk = AR.claim(O_CTX, 8192, "WkT1")
        r_xg = AR.claim(O_CTX + 8192, 4096, "xk_g")
        r_t1 = AR.claim(O_CTX + 12288, 4096, "t1sl")
        for cc in range(NCH):
            dma("syncq", AR.ap(O_CTX + cc * 1024, 1024), WkT1_d[ts(cc, 128), :],
                pwrites=(r_wk,))

        def build_shift(g, mucols, onemcols, r_dst, dst_off):
            for cc in range(NCH):
                if g == 0:
                    vop(lambda e, cc=cc: e.memset(AR.ap(O_CTX + 12288 + cc * 512, 1), 0.0),
                        (), (r_t1,))
                    sop(lambda e, cc=cc, onemcols=onemcols: e.activation(
                        out=AR.ap(O_CTX + 12288 + cc * 512 + 1, 511),
                        in_=AR.ap(O_XN + cc * S, 511), func=AF.Identity,
                        scale=c_misc[:, onemcols + cc:onemcols + cc + 1]),
                        (r_xn1, r_const), (r_t1,))
                else:
                    sop(lambda e, cc=cc, g=g, onemcols=onemcols: e.activation(
                        out=AR.ap(O_CTX + 12288 + cc * 512, 512),
                        in_=AR.ap(O_XN + cc * S + g * 512 - 1, 512), func=AF.Identity,
                        scale=c_misc[:, onemcols + cc:onemcols + cc + 1]),
                        (r_xn1, r_const), (r_t1,))
                vop(lambda e, cc=cc, g=g, mucols=mucols, dst_off=dst_off: e.scalar_tensor_tensor(
                    out=AR.ap(dst_off + cc * 512, 512),
                    in0=AR.ap(O_XN + cc * S + g * 512, 512),
                    scalar=c_misc[:, mucols + cc:mucols + cc + 1],
                    in1=AR.ap(O_CTX + 12288 + cc * 512, 512),
                    op0=ALU.mult, op1=ALU.add), (r_xn1, r_const, r_t1), (r_dst,))

        for g in range(NG):
            build_shift(g, 152, 168, r_xg, O_CTX + 8192)
            for oc in range(NCH):
                psw = (PS_A if oc % 2 == 0 else PS_B)[:, 0:512]
                rps = r_psa if oc % 2 == 0 else r_psb
                for cc in range(NCH):
                    top(lambda e, oc=oc, cc=cc, psw=psw: e.matmul(
                        psw, AR.ap(O_CTX + cc * 1024 + oc * 128, 128),
                        AR.ap(O_CTX + 8192 + cc * 512, 512),
                        start=(cc == 0), stop=(cc == NCH - 1)), (r_wk, r_xg), (rps,))
                sop(lambda e, oc=oc, g=g, psw=psw: e.activation(
                    out=AR.ap(O_AUX, 512), in_=psw, func=AF.Relu), (rps,), (r_blkg,))
                sop(lambda e, oc=oc, g=g: e.activation(
                    out=fmap(O_X1, oc, g * TG, (g + 1) * TG), in_=AR.ap(O_AUX, 512),
                    func=AF.Square), (r_blkg,), (r_ka,))
        r_wr = AR.claim(O_CTX, 8192, "WrT")
        for cc in range(NCH):
            dma("syncq", AR.ap(O_CTX + cc * 1024, 1024), WrT_d[ts(cc, 128), :],
                pwrites=(r_wr,))
        for g in range(NG):
            build_shift(g, 144, 160, r_xg, O_CTX + 8192)
            for tl in range(4):
                it = g * 4 + tl
                psx = PS_C[:, (tl % 2) * 1024:(tl % 2) * 1024 + 1024]
                for og in range(2):
                    for cc in range(NCH):
                        top(lambda e, cc=cc, og=og, tl=tl, psx=psx: e.matmul(
                            psx[:, ts(og, 512)],
                            AR.ap(O_CTX + 8192 + cc * 512 + tl * 128, 128),
                            AR.ap(O_CTX + cc * 1024 + og * 512, 512),
                            start=(cc == 0), stop=(cc == NCH - 1)), (r_wr, r_xg), (r_psc,))
                sop(lambda e, psx=psx: e.activation(out=AR.ap(O_AUX + 1024, 1024),
                    in_=psx, func=AF.Sigmoid), (r_psc,), (r_sqg,))
                dma("gpq", scr_r[ts(it, TT), :], AR.ap(O_AUX + 1024, 1024),
                    reads=(r_sqg,), writes=(r_scr,))
        r_wv = AR.claim(O_CTX, 8192, "WvT1")
        r_pf = AR.claim(O_CTX + 8192, 4096, "prefetch")
        r_x2 = AR.claim(O_CTX + 12288, 2048, "x2sl")
        for cc in range(NCH):
            dma("syncq", AR.ap(O_CTX + cc * 1024, 1024), WvT1_d[ts(cc, 128), :],
                pwrites=(r_wv,))
        for it in range(NT):
            sl = (it % 2) * 1024
            dma("syncq", AR.ap(O_CTX + 8192 + sl, 1024), scr_r[ts(it, TT), :],
                reads=(r_scr,), pwrites=(r_pf,))
            dma("syncq", AR.ap(O_CTX + 10240 + sl, 1024), scr_a1[ts(it, TT), :],
                reads=(r_scr,), pwrites=(r_pf,))
            psx = PS_C[:, sl:sl + 1024]
            for og in range(2):
                for cc in range(NCH):
                    top(lambda e, cc=cc, og=og, it=it, psx=psx: e.matmul(
                        psx[:, ts(og, 512)],
                        AR.ap(O_X1 + cc * S + it * 128, 128),
                        AR.ap(O_CTX + cc * 1024 + og * 512, 512),
                        start=(cc == 0), stop=(cc == NCH - 1)), (r_wv, r_ka), (r_psc,))
            vop(lambda e, sl=sl, psx=psx: e.tensor_mul(
                AR.ap(O_CTX + 12288 + sl, 1024), psx, AR.ap(O_CTX + 8192 + sl, 1024)),
                (r_psc, r_pf), (r_x2,))
            vop(lambda e, sl=sl: e.tensor_add(
                AR.ap(O_CTX + 12288 + sl, 1024), AR.ap(O_CTX + 12288 + sl, 1024),
                AR.ap(O_CTX + 10240 + sl, 1024)), (r_x2, r_pf), (r_x2,))
            dma("gpq", y[si, ts(it, TT), :], AR.ap(O_CTX + 12288 + sl, 1024),
                reads=(r_x2,), writes=(r_y,))

    for si in range(NSEQ):
        r_xn = phase_A(si)
        if 'B' in PHASES:
            r_ctx = phase_B(si, r_xn)
        if 'C' in PHASES:
            r_x1 = phase_C(si, r_ctx)
        if 'D' in PHASES:
            r_x1 = phase_D(si, r_x1)
        if 'E' in PHASES:
            r_xn1 = phase_E(si, r_x1)
        if 'F' in PHASES:
            phase_F(si, r_xn1, r_x1)
        if 'G' in PHASES:
            phase_G(si, r_xn1)
        if NSEQ_DEV == 1:
            break

    s.fence("gpq", [r_y.w] if r_y.w else [])
    s.fence("syncq", [r_y.w] if r_y.w else [])

    with ctx:
        with contextlib.ExitStack() as semctx:
            sems = {}
            for e in ENGS:
                if e in DMA_Q:
                    for i in range(NSEM_POOL):
                        sems[f"{e}{i}"] = semctx.enter_context(nc.semaphore(f"sem_{e}{i}"))
                else:
                    sems[e] = semctx.enter_context(nc.semaphore("sem_" + e))
            with nc.Block() as block:
                run = s.finalize(sems)
                block.tensor(lambda t: run("tensor", t))
                block.vector(lambda v: run("vector", v))
                block.scalar(lambda sc: run("scalar", sc))
                block.sync(lambda sy: run("syncq", sy))
                block.gpsimd(lambda g: run("gpq", g))
    return nc, taps


def _host_inputs(kw):
    d = {}
    d["WqT"] = np.ascontiguousarray(kw["L0_attn_Wq"].T)
    d["WkT"] = np.ascontiguousarray(kw["L0_attn_Wk"].T)
    d["WvT"] = np.ascontiguousarray(kw["L0_attn_Wv"].T)
    d["WoT"] = np.ascontiguousarray(kw["L0_attn_Wo"].T)
    bq, bk, bv = kw["L0_attn_bq"], kw["L0_attn_bk"], kw["L0_attn_bv"]
    d["b_qkv"] = np.ascontiguousarray(
        np.stack([bq, bk, bv]).reshape(3, 8, 128).transpose(2, 0, 1).reshape(128, 24))
    sl = np.asarray(kw["L0_attn_slopes"], np.float32)
    sd = np.zeros((128, 8), np.float32)
    sd1 = np.zeros((128, 8), np.float32)
    sw1 = np.zeros((128, 8), np.float32)
    sd[64, :] = sl[0::2]; sd[65, :] = sl[1::2]
    sd1[64, :] = sl[0::2]; sd1[65, :] = 1.0
    sw1[64, :] = sl[1::2]; sw1[65, :] = 1.0
    d["slopes_diag"] = sd; d["slopes_d1"] = sd1; d["slopes_s1"] = sw1
    d["WgT"] = np.ascontiguousarray(np.asarray(kw["L0_ffn_Wg"]).T.reshape(8, 128, 8).transpose(1, 0, 2).reshape(128, 64))
    d["W1"] = np.ascontiguousarray(kw["L0_ffn_W1"])
    d["W2"] = np.ascontiguousarray(kw["L0_ffn_W2"])
    d["lnsc_c"] = np.ascontiguousarray(
        np.asarray(kw["L0_ffn_ln_scale"]).reshape(8, 8, 128).transpose(2, 0, 1).reshape(128, 64))
    d["lnb_c"] = np.ascontiguousarray(
        np.asarray(kw["L0_ffn_ln_bias"]).reshape(8, 8, 128).transpose(2, 0, 1).reshape(128, 64))
    d["b1c"] = np.ascontiguousarray(
        np.asarray(kw["L0_ffn_b1"]).reshape(8, 2, 128).transpose(2, 0, 1).reshape(128, 16))
    d["biasmm0"] = np.ascontiguousarray(
        np.concatenate([kw["L0_ffn_b2"], np.asarray(kw["L0_attn_bo"])[None]], axis=0))
    d["qkvT"] = np.ascontiguousarray(kw["L1_attn_qkv_w"].T)
    d["outwT"] = np.ascontiguousarray(kw["L1_attn_out_w"].T)
    d["biasmm1"] = np.ascontiguousarray(np.asarray(kw["L1_attn_out_b"])[None])
    d["temp16"] = np.ascontiguousarray(kw["L1_attn_temp"])
    d["WrT"] = np.ascontiguousarray(kw["L1_ffn_W_r"].T)
    d["WkT1"] = np.ascontiguousarray(kw["L1_ffn_W_k"].T)
    d["WvT1"] = np.ascontiguousarray(kw["L1_ffn_W_v"].T)
    d["mur_c"] = np.ascontiguousarray(np.asarray(kw["L1_ffn_mu_r"]).reshape(8, 128).T)
    d["muk_c"] = np.ascontiguousarray(np.asarray(kw["L1_ffn_mu_k"]).reshape(8, 128).T)
    d["ident"] = np.eye(128, dtype=np.float32)
    aug2 = np.zeros((128, S), np.float32)
    aug2[64, :] = np.arange(S, dtype=np.float32)
    aug2[65, :] = 1.0
    aug2[66, :] = 1.0
    d["aug2"] = aug2
    d["onesq"] = np.ones((128, 128), np.float32)
    e2v = np.zeros((128, 2), np.float32); e2v[0:64, 0] = 1.0; e2v[64:128, 1] = 1.0
    d["e2"] = e2v
    mb_ = np.zeros((16, C), np.float32)
    for h in range(16):
        mb_[h, h * 64:(h + 1) * 64] = -1.0
    d["maskblk"] = mb_
    return {k: np.ascontiguousarray(np.asarray(v, dtype=np.float32)) for k, v in d.items()}


def kernel(**kw):
    shared = _host_inputs(kw)
    x = np.ascontiguousarray(np.asarray(kw["L0_attn_x"], dtype=np.float32))
    nc, taps = build_program(DEBUG_TAPS)
    in_maps = []
    for core in range(8):
        m = dict(shared)
        m["x0"] = np.ascontiguousarray(x[2 * core:2 * core + 2])
        in_maps.append(m)
    res = run_bass_kernel_spmd(nc, in_maps, core_ids=list(range(8)))
    out = np.concatenate([r["y"] for r in res.results], axis=0)
    return out.reshape(16, S, C)


# version 11 (final)
# speedup vs baseline: 1.0977x; 1.0977x over previous
import contextlib
import numpy as np
import concourse.bass as bass
import concourse.mybir as mybir
from concourse.bass_utils import run_bass_kernel_spmd

F32 = mybir.dt.float32
AF = mybir.ActivationFunctionType
ALU = mybir.AluOpType
AX = mybir.AxisListType

S, C, NSEQ = 2048, 1024, 2
NH, DH = 16, 64
NE, HID = 8, 256
TT, TG = 128, 512
NT, NG, NCH = S // TT, S // TG, C // 128
LN_EPS = 1e-5
ARENA_N = 51392
O_XN, O_CTX, O_X1, O_AUX = 0, 16384, 32768, 49152

ENGS = ("tensor", "vector", "scalar", "syncq", "gpq")
DMA_Q = ("syncq", "gpq")

DEBUG_TAPS = ()
PTAP = 2
PARANOID = True
NPAIRS = 8
PHASES = 'ABCDEFG'
NSEQ_DEV = 2
SKIP_HEADS = False
SKIP_NORM = False
SKIP_CTX = False
BARRIER = False


class Rgn:
    __slots__ = ("name", "w", "rs", "pw")

    def __init__(self, name):
        self.name = name
        self.w = None
        self.rs = []
        self.pw = []


class _Op:
    __slots__ = ("fn", "waits", "needs_inc", "dma")


NSEM_POOL = 8


class Sched:

    def __init__(self):
        self.ops = {e: [] for e in ENGS}
        self.dmas = {e: [] for e in DMA_Q}
        self.ndma = {e: 0 for e in DMA_Q}

    def op(self, eng, fn, reads=(), writes=(), pwrites=(), force_inc=False):
        o = _Op()
        o.fn = fn
        o.needs_inc = force_inc or (eng in DMA_Q)
        o.waits = []
        idx = len(self.ops[eng])
        tok = (eng, idx)
        if eng in DMA_Q:
            n = self.ndma[eng]
            o.dma = (n % NSEM_POOL, n // NSEM_POOL + 1)
            self.ndma[eng] = n + 1
        else:
            o.dma = None
        edges = []
        for r in reads:
            if r.w is not None:
                edges.append((r.w, True))
            for t in r.pw:
                edges.append((t, True))
        for w_ in writes:
            if w_.w is not None:
                edges.append((w_.w, False))
            for t in w_.pw:
                edges.append((t, False))
            for t in w_.rs:
                edges.append((t, False))
        for w_ in pwrites:
            if w_.w is not None:
                edges.append((w_.w, False))
            for t in w_.rs:
                edges.append((t, False))
        for (we, wi), is_raw in edges:
            if we == eng == "tensor" and not is_raw:
                continue
            if (not PARANOID) and we == eng and eng not in DMA_Q and not is_raw \
                    and eng != "scalar":
                continue
            if (we, wi) == tok:
                continue
            o.waits.append((we, wi))
            self.ops[we][wi].needs_inc = True
        self.ops[eng].append(o)
        for r in reads:
            r.rs.append(tok)
        for w_ in writes:
            w_.w = tok
            w_.rs = []
            w_.pw = []
        for w_ in pwrites:
            w_.pw.append(tok)
        return tok

    def barrier(self):
        toks = []
        for e in ENGS:
            for i in range(len(self.ops[e]) - 1, -1, -1):
                o = self.ops[e][i]
                if (e in DMA_Q and o.dma is not None) or (e not in DMA_Q and o.fn is not None):
                    toks.append((e, i))
                    break
        for e in ENGS:
            other = [t for t in toks if t[0] != e]
            if other:
                self.fence(e, other)

    def fence(self, eng, tokens):
        o = _Op()
        o.fn = None
        o.needs_inc = False
        o.dma = None
        o.waits = list(tokens)
        for (we, wi) in tokens:
            self.ops[we][wi].needs_inc = True
        self.ops[eng].append(o)

    def finalize(self, sems):
        incval = {}
        for e in ENGS:
            if e in DMA_Q:
                continue
            v, vals = 0, []
            for o in self.ops[e]:
                if o.needs_inc:
                    v += 1
                vals.append(v)
            incval[e] = vals

        def resolve(we, wi):
            if we in DMA_Q:
                slot, cnt = self.ops[we][wi].dma
                return (f"{we}{slot}", 16 * cnt)
            return (we, incval[we][wi])

        hw = {e: {} for e in ENGS}

        def run_engine(e, eng_obj):
            my_hw = hw[e]
            for o in self.ops[e]:
                best = {}
                for (we, wi) in o.waits:
                    sname, v = resolve(we, wi)
                    assert v > 0, f"wait on op without inc: {we}[{wi}]"
                    if v > best.get(sname, 0):
                        best[sname] = v
                if o.dma is not None:
                    slot, cnt = o.dma
                    if cnt > 1:
                        sname = f"{e}{slot}"
                        if best.get(sname, 0) < 16 * (cnt - 1):
                            best[sname] = 16 * (cnt - 1)
                for sname in sorted(best):
                    v = best[sname]
                    if my_hw.get(sname, 0) >= v:
                        continue
                    eng_obj.wait_ge(sems[sname], v)
                    my_hw[sname] = v
                if o.fn is not None:
                    ins = o.fn(eng_obj)
                    if o.dma is not None:
                        slot, cnt = o.dma
                        ins.then_inc(sems[f"{e}{slot}"], 16)
                    elif o.needs_inc:
                        ins.then_inc(sems[e], 1)
        return run_engine


class Arena:

    def __init__(self, handle):
        self.h = handle
        self.live = []

    def claim(self, off, size, name):
        end = off + size
        rgn = Rgn(name)
        keep = []
        for (o0, e0, r) in self.live:
            if e0 <= off or o0 >= end:
                keep.append((o0, e0, r))
            else:
                if r.w is not None:
                    rgn.rs.append(r.w)
                rgn.rs.extend(r.rs)
                rgn.rs.extend(r.pw)
                if o0 < off:
                    keep.append((o0, off, r))
                if e0 > end:
                    keep.append((end, e0, r))
        keep.append((off, end, rgn))
        self.live = keep
        return rgn

    def ap(self, off, size, p0=0, p1=128):
        return self.h[p0:p1, off:off + size]


def ts(i, n):
    return slice(i * n, (i + 1) * n)


def build_program(debug_taps=()):
    nc = bass.Bass("TRN2", target_bir_lowering=False, debug=False,
                   dynamic_dma_scratch_size=4096)

    def din(name, shape):
        return nc.dram_tensor(name, list(shape), F32, kind="ExternalInput").ap()

    def dout(name, shape):
        return nc.dram_tensor(name, list(shape), F32, kind="ExternalOutput").ap()

    x0 = din("x0", (NSEQ, S, C))
    WqT = din("WqT", (128, 8192)); WkT = din("WkT", (128, 8192))
    WvT = din("WvT", (128, 8192))
    WoT_d = din("WoT", (C, C))
    b_qkv = din("b_qkv", (128, 24))
    slopes_diag = din("slopes_diag", (128, 8))
    slopes_d1 = din("slopes_d1", (128, 8))
    slopes_s1 = din("slopes_s1", (128, 8))
    WgT_d = din("WgT", (128, 64))
    W1_d = din("W1", (NE, C, HID)); W2_d = din("W2", (NE, HID, C))
    lnsc_c = din("lnsc_c", (128, 64))
    lnb_c = din("lnb_c", (128, 64))
    b1c = din("b1c", (128, 16))
    biasmm0 = din("biasmm0", (9, C))
    qkvT_d = din("qkvT", (C, C)); outwT_d = din("outwT", (C, C))
    biasmm1 = din("biasmm1", (1, C))
    temp16 = din("temp16", (16, 1))
    WrT_d = din("WrT", (C, C)); WkT1_d = din("WkT1", (C, C)); WvT1_d = din("WvT1", (C, C))
    mur_c = din("mur_c", (128, 8)); muk_c = din("muk_c", (128, 8))
    ident_d = din("ident", (128, 128))
    aug2_d = din("aug2", (128, S))
    onesq_d = din("onesq", (128, 128))
    e2_d = din("e2", (128, 2))
    maskblk_d = din("maskblk", (16, C))
    y = dout("y", (NSEQ, S, C))
    scr_a1 = nc.dram_tensor("scr_a1", [S, C], F32, kind="Internal").ap()
    scr_sh = nc.dram_tensor("scr_sh", [64, 2048], F32, kind="Internal").ap()
    scr_r = nc.dram_tensor("scr_r", [S, C], F32, kind="Internal").ap()

    ctx = contextlib.ExitStack()

    def sbt(name, p, n):
        return ctx.enter_context(nc.sbuf_tensor(name, [p, n], F32))

    c_ident = sbt("c_ident", 128, 128)
    c_ones = sbt("c_ones", 128, 128)
    c_rows = sbt("c_rows", 128, 1024)
    c_misc = sbt("c_misc", 128, 256)
    c_moe = sbt("c_moe", 128, 128)
    c_temp = sbt("c_temp", 16, 1)
    st_cols = sbt("st_cols", 128, 64)
    c_eps = sbt("c_eps", 128, 1)
    c_eps8 = sbt("c_eps8", 128, 1)
    st_aff = sbt("st_aff", 128, 64)
    st_rows = sbt("st_rows", 128, 2048)
    sm = sbt("sm", 128, 1024)
    AR = Arena(sbt("ARENA", 128, ARENA_N))

    PS_A = ctx.enter_context(nc.psum_tensor("PS_A", [128, 1024], F32))
    PS_B = ctx.enter_context(nc.psum_tensor("PS_B", [128, 1024], F32))
    PS_C = ctx.enter_context(nc.psum_tensor("PS_C", [128, 2048], F32))

    s = Sched()
    r_const = Rgn("const")
    r_stats = Rgn("stats")
    r_aff = Rgn("aff")
    r_rows = Rgn("rows")
    r_sm = Rgn("sm")
    r_psa = Rgn("PS_A"); r_psb = Rgn("PS_B"); r_psc = Rgn("PS_C")
    r_y = Rgn("y"); r_scr = Rgn("scr")

    def dma(q, out, in_, reads=(), writes=(), pwrites=()):
        return s.op("syncq", lambda e, o=out, i=in_: e.dma_start(out=o, in_=i),
                    reads=reads, writes=writes, pwrites=pwrites)

    def vop(fn, reads=(), writes=()):
        return s.op("vector", fn, reads=reads, writes=writes)

    def sop(fn, reads=(), writes=()):
        return s.op("scalar", fn, reads=reads, writes=writes)

    def top(fn, reads=(), writes=(), force_inc=False):
        return s.op("tensor", fn, reads=reads, writes=writes, force_inc=force_inc)

    for out_t, in_ap in [
        (c_ident[:, :], ident_d), (c_ones[:, :], onesq_d),
        (c_rows[0:16, :], maskblk_d), (c_rows[32:41, :], biasmm0), (c_rows[64:65, :], biasmm1),
        (c_misc[:, 0:24], b_qkv), (c_misc[:, 32:40], slopes_diag),
        (c_misc[:, 40:48], slopes_d1), (c_misc[:, 48:56], slopes_s1),
        (c_misc[:, 64:128], WgT_d),
        (c_misc[:, 128:144], b1c), (c_misc[:, 144:152], mur_c), (c_misc[:, 152:160], muk_c),
        (c_misc[:, 184:186], e2_d),
        (c_moe[:, 0:64], lnsc_c), (c_moe[:, 64:128], lnb_c),
        (c_temp[:, :], temp16),
    ]:
        dma("syncq", out_t, in_ap, pwrites=(r_const,))

    SQ8 = float(2.0 ** -1.5)
    vop(lambda e: e.tensor_scalar(out=c_misc[:, 24:32], in0=c_misc[:, 0:8],
        scalar1=SQ8, scalar2=None, op0=ALU.mult), (r_const,), (r_const,))
    vop(lambda e: e.tensor_scalar(out=c_misc[:, 176:184], in0=c_misc[:, 8:16],
        scalar1=SQ8, scalar2=None, op0=ALU.mult), (r_const,), (r_const,))
    vop(lambda e: e.memset(c_eps[:, :], LN_EPS), (), (r_const,))
    vop(lambda e: e.memset(c_eps8[:, :], 1e-8), (), (r_const,))
    vop(lambda e: e.scalar_tensor_tensor(out=c_misc[:, 160:168], in0=c_misc[:, 144:152],
        scalar=-1.0, in1=c_ones[:, 0:8], op0=ALU.mult, op1=ALU.add), (r_const,), (r_const,))
    vop(lambda e: e.scalar_tensor_tensor(out=c_misc[:, 168:176], in0=c_misc[:, 152:160],
        scalar=-1.0, in1=c_ones[:, 0:8], op0=ALU.mult, op1=ALU.add), (r_const,), (r_const,))

    taps = {}

    def mktap(name, shape):
        if name in debug_taps:
            taps[name] = dout("dbg_" + name, shape)
            return taps[name]
        return None

    def fmap(off, cc, t0, t1):
        return AR.ap(off + cc * S + t0, t1 - t0)

    def phase_A(si):
        r_xn = AR.claim(O_XN, C * NCH * 0 + 16384, "xn0")
        r_xsl = [AR.claim(O_CTX + i * C, C, f"xsl{i}") for i in range(2)]
        r_xt = [AR.claim(O_AUX + i * C, C, f"xt{i}") for i in range(2)]
        cS = st_cols
        for it in range(NT):
            xt = AR.ap(O_AUX + (it % 2) * C, C)
            rx = r_xt[it % 2]
            sl = AR.ap(O_CTX + (it % 2) * C, C)
            rsl = r_xsl[it % 2]
            sq = AR.ap(O_CTX + ((it + 1) % 2) * C, C)
            rsq = r_xsl[(it + 1) % 2]
            dma("syncq", xt, x0[si, ts(it, TT), :], writes=(rx,))
            vop(lambda e, xt=xt: e.reduce_sum(cS[:, 0:1], xt, axis=AX.X), (rx,), (r_stats,))
            sop(lambda e, xt=xt, sq=sq: e.activation(out=sq, in_=xt, func=AF.Square,
                accum_out=cS[:, 1:2]), (rx,), (rsq, r_stats))
            vop(lambda e: e.tensor_scalar(out=cS[:, 2:3], in0=cS[:, 0:1],
                scalar1=1.0 / C, scalar2=None, op0=ALU.mult), (r_stats,), (r_stats,))
            vop(lambda e: e.tensor_scalar(out=cS[:, 3:4], in0=cS[:, 1:2],
                scalar1=1.0 / C, scalar2=None, op0=ALU.mult), (r_stats,), (r_stats,))
            vop(lambda e: e.scalar_tensor_tensor(out=cS[:, 4:5], in0=cS[:, 2:3],
                scalar=cS[:, 2:3], in1=cS[:, 3:4], op0=ALU.mult, op1=ALU.subtract),
                (r_stats,), (r_stats,))
            vop(lambda e: e.scalar_tensor_tensor(out=cS[:, 16:17], in0=cS[:, 4:5],
                scalar=-1.0, in1=c_eps[:, 0:1], op0=ALU.mult, op1=ALU.add), (r_stats, r_const), (r_stats,))
            sop(lambda e: e.activation(out=cS[:, 5:6], in_=cS[:, 16:17], func=AF.Sqrt),
                (r_stats,), (r_stats,))
            vop(lambda e: e.reciprocal(out=cS[:, 6:7], in_=cS[:, 5:6]), (r_stats,), (r_stats,))
            vop(lambda e: e.scalar_tensor_tensor(out=cS[:, 7:8], in0=cS[:, 2:3],
                scalar=-1.0, in1=cS[:, 6:7], op0=ALU.mult, op1=ALU.mult),
                (r_stats,), (r_stats,))
            sop(lambda e, xt=xt, sl=sl: e.activation(out=sl, in_=xt, func=AF.Identity,
                scale=cS[:, 6:7], bias=cS[:, 7:8]), (rx, r_stats), (rsl,))
            if it == 0:
                t1 = mktap(f"xntm0_{si}", (128, C))
                if t1 is not None:
                    dma("gpq", t1, sl, reads=(rsl,))
                t2 = mktap(f"stats0_{si}", (128, 20))
                if t2 is not None:
                    dma("gpq", t2, cS[:, 0:20], reads=(r_stats,))
            vop(lambda e, sl=sl: e.reduce_sum(cS[:, 8:9], sl, axis=AX.X), (rsl,), (r_stats,))
            sop(lambda e, sl=sl, sq=sq: e.activation(out=sq, in_=sl, func=AF.Square,
                accum_out=cS[:, 9:10]), (rsl,), (rsq, r_stats))
            vop(lambda e: e.tensor_scalar(out=cS[:, 10:11], in0=cS[:, 8:9],
                scalar1=1.0 / C, scalar2=None, op0=ALU.mult), (r_stats,), (r_stats,))
            vop(lambda e: e.tensor_scalar(out=cS[:, 11:12], in0=cS[:, 9:10],
                scalar1=1.0 / C, scalar2=None, op0=ALU.mult), (r_stats,), (r_stats,))
            vop(lambda e: e.scalar_tensor_tensor(out=cS[:, 12:13], in0=cS[:, 10:11],
                scalar=cS[:, 10:11], in1=cS[:, 11:12], op0=ALU.mult, op1=ALU.subtract),
                (r_stats,), (r_stats,))
            vop(lambda e: e.scalar_tensor_tensor(out=cS[:, 17:18], in0=cS[:, 12:13],
                scalar=-1.0, in1=c_eps[:, 0:1], op0=ALU.mult, op1=ALU.add), (r_stats, r_const), (r_stats,))
            sop(lambda e: e.activation(out=cS[:, 13:14], in_=cS[:, 17:18], func=AF.Sqrt),
                (r_stats,), (r_stats,))
            vop(lambda e: e.reciprocal(out=cS[:, 14:15], in_=cS[:, 13:14]), (r_stats,), (r_stats,))
            vop(lambda e, it=it: e.scalar_tensor_tensor(out=st_aff[:, 4 * it:4 * it + 1],
                in0=cS[:, 6:7], scalar=1.0, in1=cS[:, 14:15], op0=ALU.bypass, op1=ALU.mult),
                (r_stats,), (r_aff,))
            vop(lambda e: e.tensor_sub(cS[:, 15:16], cS[:, 7:8], cS[:, 10:11]),
                (r_stats,), (r_stats,))
            vop(lambda e, it=it: e.scalar_tensor_tensor(out=st_aff[:, 4 * it + 1:4 * it + 2],
                in0=cS[:, 15:16], scalar=1.0, in1=cS[:, 14:15], op0=ALU.bypass, op1=ALU.mult),
                (r_stats,), (r_aff,))
            for cc in range(NCH):
                pst = (PS_A if cc < 4 else PS_B)[:, ts(cc % 4, 128)]
                rps = r_psa if cc < 4 else r_psb
                top(lambda e, sl=sl, cc=cc, pst=pst: e.transpose(
                    pst, sl[:, ts(cc, 128)], c_ident[:, :]), (rsl, r_const), (rps,))
            sop(lambda e, it=it: e.activation(
                out=AR.ap(O_XN, 8 * S).rearrange("p (c t) -> p c t", c=NCH)[:, 0:4, ts(it, TT)],
                in_=PS_A[:, 0:512].rearrange("p (c t) -> p c t", c=4), func=AF.Copy),
                (r_psa,), (r_xn,))
            sop(lambda e, it=it: e.activation(
                out=AR.ap(O_XN, 8 * S).rearrange("p (c t) -> p c t", c=NCH)[:, 4:8, ts(it, TT)],
                in_=PS_B[:, 0:512].rearrange("p (c t) -> p c t", c=4), func=AF.Copy),
                (r_psb,), (r_xn,))
        tap = mktap(f"xn0_{si}", (C, S))
        if tap is not None:
            for cc in range(NCH):
                dma("gpq", tap[ts(cc, 128), :], fmap(O_XN, cc, 0, S), reads=(r_xn,))
        tap = mktap(f"aff_{si}", (128, 64))
        if tap is not None:
            dma("gpq", tap, st_aff[:, :], reads=(r_aff,))

        for g in range(NG):
            for cc in range(NCH):
                top(lambda e, g=g, cc=cc: e.matmul(
                    PS_C[0:8, ts(g, TG)], c_misc[:, 64 + cc * 8: 72 + cc * 8],
                    fmap(O_XN, cc, g * TG, (g + 1) * TG),
                    start=(cc == 0), stop=(cc == NCH - 1)),
                    (r_xn, r_const), (r_psc,))
        sop(lambda e: e.activation(out=st_rows[0:8, :], in_=PS_C[0:8, :], func=AF.Copy),
            (r_psc,), (r_rows,))
        for it in range(NT):
            pst = (PS_A if it % 8 < 4 else PS_B)[:, ts(it % 4, 16)]
            rps = r_psa if it % 8 < 4 else r_psb
            top(lambda e, it=it, pst=pst: e.transpose(
                pst[:, 0:8], st_rows[0:8, ts(it, TT)], c_ident[0:8, 0:8]),
                (r_rows, r_const), (rps,))
            sop(lambda e, it=it, pst=pst: e.activation(out=sm[:, ts(it, 8)], in_=pst[:, 0:8],
                func=AF.Copy), (rps,), (r_sm,))
            vop(lambda e, it=it: e.reduce_max(sm[:, 128 + it:129 + it], sm[:, ts(it, 8)],
                axis=AX.X), (r_sm,), (r_sm,))
            vop(lambda e, it=it: e.tensor_scalar(out=sm[:, 256 + it * 8: 264 + it * 8],
                in0=sm[:, ts(it, 8)], scalar1=sm[:, 128 + it:129 + it], scalar2=None,
                op0=ALU.is_equal), (r_sm,), (r_sm,))
            pst2 = (PS_A if it % 8 < 4 else PS_B)[:, 512 + (it % 4) * 128: 640 + (it % 4) * 128]
            top(lambda e, it=it, pst2=pst2: e.transpose(
                pst2[0:8, :], sm[:, 256 + it * 8: 264 + it * 8], c_ident[:, :]),
                (r_sm, r_const), (rps,))
            sop(lambda e, it=it, pst2=pst2: e.activation(out=st_rows[32:40, ts(it, TT)],
                in_=pst2[0:8, :], func=AF.Copy), (rps,), (r_rows,))
        dma("syncq", st_rows[40:41, :], aug2_d[65:66, :], writes=(r_rows,))
        tap = mktap(f"mask_{si}", (9, S))
        if tap is not None:
            dma("gpq", tap, st_rows[32:41, :], reads=(r_rows,))
        return r_xn


    B_QA, B_QB, B_KA, B_KB = O_X1, O_X1 + 2048, O_X1 + 4096, O_X1 + 6144
    B_SX, B_VT, B_PT, B_WS, B_SP = (O_X1 + 8192, O_X1 + 8704, O_X1 + 13856,
                                    O_X1 + 10784, O_X1 + 15904)
    SQ8 = float(2.0 ** -1.5)

    def phase_B(si, r_xn):
        r_aug = AR.claim(O_AUX, 2048, "aug2sb")
        r_qA = AR.claim(B_QA, 2048, "qA"); r_qB = AR.claim(B_QB, 2048, "qB")
        r_kA = AR.claim(B_KA, 2048, "kA"); r_kB = AR.claim(B_KB, 2048, "kB")
        r_sx = AR.claim(B_SX, 512, "slabX")
        r_vt = AR.claim(B_VT, 2080, "vT")
        r_pts = [AR.claim(B_PT + i_ * 512, 512, f"PT{i_}") for i_ in range(4)]
        r_ws = AR.claim(B_WS, 3072, "WS")
        r_sp = AR.claim(B_SP, 480, "SP")
        r_ctx = AR.claim(O_CTX, 16384, "ctx")
        aug_sb = AR.ap(O_AUX, 2048)
        dma("syncq", aug_sb[64:66, :], aug2_d[64:66, :], writes=(r_aug,))
        dma("syncq", AR.ap(B_QA, 2048, 64, 65), aug2_d[65:66, :], writes=(r_qA,))
        dma("syncq", AR.ap(B_QB, 2048, 64, 65), aug2_d[65:66, :], writes=(r_qB,))
        vt3 = AR.ap(B_VT, 2080).rearrange("p (t u) -> p t u", u=130)
        vop(lambda e: e.memset(vt3[:, :, 64:65], 1.0), (), (r_vt,))
        vop(lambda e: e.memset(vt3[:, :, 129:130], 1.0), (), (r_vt,))
        cell = lambda i: AR.ap(B_SP + i, 1, 64, 66)
        vop(lambda e: e.tensor_copy(AR.ap(B_SP + 0, 1, 64, 65), c_ones[64:65, 0:1]),
            (r_const,), (r_sp,))
        vop(lambda e: e.tensor_copy(AR.ap(B_SP + 1, 1, 64, 65), c_ones[64:65, 0:1]),
            (r_const,), (r_sp,))

        wsq = AR.ap(B_WS, 1024); wsk = AR.ap(B_WS + 1024, 1024); wsv = AR.ap(B_WS + 2048, 1024)

        def load_ws(p):
            for wsi, W_d in enumerate((WqT, WkT, WvT)):
                dma("syncq", AR.ap(B_WS + wsi * 1024, 1024),
                    W_d[:, ts(p, 1024)], reads=(), pwrites=(r_ws,))

        load_ws(0)
        for p in range(NPAIRS):
            for wi, (ws_ap, bcol, dstA, r_dA, r_dB, dstB_off) in enumerate((
                    (wsq, c_misc[:, 24 + p:25 + p], B_QA, r_qA, r_qB, B_QB),
                    (wsk, c_misc[:, 176 + p:177 + p], B_KA, r_kA, r_kB, B_KB),
                    (wsv, c_misc[:, 16 + p:17 + p], None, None, None, None))):
                scale = SQ8 if wi < 2 else 1.0
                for g in range(NG):
                    psw = (PS_A if g % 2 == 0 else PS_B)[:, 0:512]
                    rps = r_psa if g % 2 == 0 else r_psb
                    for cc in range(NCH):
                        top(lambda e, ws_ap=ws_ap, cc=cc, g=g, psw=psw: e.matmul(
                            psw, ws_ap[:, ts(cc, 128)], fmap(O_XN, cc, g * TG, (g + 1) * TG),
                            start=(cc == 0), stop=(cc == NCH - 1)),
                            (r_ws, r_xn), (rps,))
                    if wi < 2:
                        sop(lambda e, psw=psw, g=g, dstA=dstA, bcol=bcol, scale=scale:
                            e.activation(out=AR.ap(dstA + g * TG, TG, 0, 64),
                                         in_=psw[0:64, :], func=AF.Identity,
                                         scale=scale, bias=bcol[0:64, :]),
                            (rps, r_const), (r_dA,))
                        sop(lambda e, psw=psw, bcol=bcol, scale=scale:
                            e.activation(out=AR.ap(B_SX, 512, 64, 128),
                                         in_=psw[64:128, :], func=AF.Identity,
                                         scale=scale, bias=bcol[64:128, :]),
                            (rps, r_const), (r_sx,))
                        dma("gpq", AR.ap(dstB_off + g * TG, TG, 0, 64),
                            AR.ap(B_SX, 512, 64, 128), reads=(r_sx,), writes=(r_dB,))
                        sop(lambda e, psw=psw, bcol=bcol, scale=scale:
                            e.activation(out=AR.ap(B_PT, 512), in_=psw,
                                         func=AF.Square, scale=scale, bias=bcol),
                            (rps, r_const), (r_pts[0],))
                        top(lambda e, g=g: e.matmul(
                            PS_C[64:66, ts(g, TG)], c_misc[:, 184:186], AR.ap(B_PT, 512),
                            start=True, stop=True, tile_position=(0, 64)),
                            (r_pts[0], r_const), (r_psc,))
                    else:
                        sop(lambda e, psw=psw, bcol=bcol: e.activation(
                            out=AR.ap(B_SX, 512), in_=psw, func=AF.Identity,
                            scale=1.0, bias=bcol), (rps, r_const), (r_sx,))
                        if p == PTAP and g == 0:
                            t7 = mktap(f"bsx_{si}", (128, 512))
                            if t7 is not None:
                                dma("gpq", t7, AR.ap(B_SX, 512), reads=(r_sx,))
                        pvb = PS_B if g % 2 == 0 else PS_A
                        rpv = r_psb if g % 2 == 0 else r_psa
                        if p == PTAP and g == 0:
                            t7s = mktap(f"bstale_{si}", (128, 512))
                            if t7s is not None:
                                sop(lambda e, pvb=pvb: e.activation(out=AR.ap(B_PT + 512, 512),
                                    in_=pvb[:, 512:1024], func=AF.Copy), (rpv,), (r_pts[1],))
                                dma("gpq", t7s, AR.ap(B_PT + 512, 512), reads=(r_pts[1],))
                            t7i = mktap(f"bid_{si}", (128, 128))
                            if t7i is not None:
                                dma("gpq", t7i, c_ident[:, :], reads=(r_const,))
                        for tc in range(4):
                            top(lambda e, tc=tc, pvb=pvb: e.transpose(
                                pvb[:, 512 + tc * 128:640 + tc * 128],
                                AR.ap(B_SX + tc * 128, 128), c_ident[:, :]),
                                (r_sx, r_const), (rpv,))
                        sop(lambda e, g=g, pvb=pvb: e.activation(
                            out=AR.ap(B_VT + g * 520, 520).rearrange(
                                "p (t u v) -> p t u v", t=4, v=65)[:, :, :, 0:64],
                            in_=pvb[:, 512:1024].rearrange(
                                "p (t u v) -> p t u v", t=4, v=64),
                            func=AF.Copy), (rpv,), (r_vt,))
                        if p == PTAP and g == 0:
                            t7b = mktap(f"bpv_{si}", (128, 512))
                            if t7b is not None:
                                sop(lambda e, pvb=pvb: e.activation(out=AR.ap(B_PT, 512),
                                    in_=pvb[:, 512:1024], func=AF.Copy), (rpv,), (r_pts[0],))
                                dma("gpq", t7b, AR.ap(B_PT, 512), reads=(r_pts[0],))
                            t7c = mktap(f"bvt2_{si}", (128, 520))
                            if t7c is not None:
                                dma("gpq", t7c, AR.ap(B_VT, 520), reads=(r_vt,))
                if wi == 0:
                    vop(lambda e: e.reduce_max(cell(2), PS_C[64:66, :], axis=AX.X),
                        (r_psc,), (r_sp,))
                elif wi == 1:
                    vop(lambda e: e.reduce_max(cell(3), PS_C[64:66, :], axis=AX.X),
                        (r_psc,), (r_sp,))
            vop(lambda e: e.tensor_add(cell(4), cell(2), cell(3)), (r_sp,), (r_sp,))
            vop(lambda e: e.tensor_scalar(out=cell(5), in0=cell(4), scalar1=-0.5,
                scalar2=None, op0=ALU.mult), (r_sp,), (r_sp,))
            vop(lambda e, p=p: e.scalar_tensor_tensor(out=cell(6),
                in0=c_misc[64:66, 32 + p:33 + p], scalar=-2047.0, in1=cell(5),
                op0=ALU.mult, op1=ALU.add), (r_sp, r_const), (r_sp,))
            dma("gpq", AR.ap(B_SP + 0, 1, 65, 66), AR.ap(B_SP + 6, 1, 64, 65),
                reads=(r_sp,), writes=(r_sp,))
            dma("gpq", AR.ap(B_SP + 1, 1, 65, 66), AR.ap(B_SP + 6, 1, 65, 66),
                reads=(r_sp,), writes=(r_sp,))
            dma("syncq", AR.ap(B_QA, 2048, 65, 66), aug2_d[65:66, :], writes=(r_qA,))
            dma("syncq", AR.ap(B_QB, 2048, 65, 66), aug2_d[65:66, :], writes=(r_qB,))
            vop(lambda e: e.tensor_scalar(out=AR.ap(B_QA, 2048, 64, 66),
                in0=AR.ap(B_QA, 2048, 64, 66), scalar1=cell(0), scalar2=None,
                op0=ALU.mult), (r_sp, r_qA), (r_qA,))
            vop(lambda e: e.tensor_scalar(out=AR.ap(B_QB, 2048, 64, 66),
                in0=AR.ap(B_QB, 2048, 64, 66), scalar1=cell(1), scalar2=None,
                op0=ALU.mult), (r_sp, r_qB), (r_qB,))
            vop(lambda e, p=p: e.tensor_scalar(out=AR.ap(B_KA, 2048, 64, 66),
                in0=aug_sb[64:66, :], scalar1=c_misc[64:66, 40 + p:41 + p], scalar2=None,
                op0=ALU.mult), (r_aug, r_const), (r_kA,))
            vop(lambda e, p=p: e.tensor_scalar(out=AR.ap(B_KB, 2048, 64, 66),
                in0=aug_sb[64:66, :], scalar1=c_misc[64:66, 48 + p:49 + p], scalar2=None,
                op0=ALU.mult), (r_aug, r_const), (r_kB,))
            if p == PTAP:
                t1 = mktap(f"baug_{si}", (8, S))
                if t1 is not None:
                    dma("gpq", t1[0:2, :], AR.ap(B_QA, 2048, 64, 66), reads=(r_qA,))
                    dma("gpq", t1[2:4, :], AR.ap(B_KA, 2048, 64, 66), reads=(r_kA,))
                    dma("gpq", t1[4:6, :], AR.ap(B_QB, 2048, 64, 66), reads=(r_qB,))
                    dma("gpq", t1[6:8, :], AR.ap(B_KB, 2048, 64, 66), reads=(r_kB,))
                t2 = mktap(f"bcell_{si}", (2, 8))
                if t2 is not None:
                    dma("gpq", t2, AR.ap(B_SP, 8, 64, 66), reads=(r_sp,))
                t3 = mktap(f"bq_{si}", (66, S))
                if t3 is not None:
                    dma("gpq", t3, AR.ap(B_QA, 2048, 0, 66), reads=(r_qA,))
                t4 = mktap(f"bvt_{si}", (128, 260))
                if t4 is not None:
                    dma("gpq", t4, AR.ap(B_VT, 260), reads=(r_vt,))
            if BARRIER:
                s.barrier()
            if p + 1 < NPAIRS:
                load_ws(p + 1)
            for a in (() if SKIP_HEADS else (0, 1)):
                QO = B_QA if a == 0 else B_QB
                KO = B_KA if a == 0 else B_KB
                r_q = r_qA if a == 0 else r_qB
                r_k = r_kA if a == 0 else r_kB
                def emit_st(tk, gh, QO=QO, KO=KO, r_q=r_q, r_k=r_k):
                    ps = PS_A if gh == 0 else PS_B
                    rps = r_psa if gh == 0 else r_psb
                    for u in range(2):
                        tq0 = gh * 1024 + u * 512
                        top(lambda e, tk=tk, u=u, ps=ps, tq0=tq0, QO=QO, KO=KO:
                            e.matmul(ps[:, ts(u, 512)],
                                     AR.ap(KO + tk * 128, 128, 0, 66),
                                     AR.ap(QO + tq0, 512, 0, 66),
                                     start=True, stop=True),
                            (r_q, r_k), (rps,))
                    for u in range(2):
                        sop(lambda e, ps=ps, gh=gh, u=u: e.activation(
                            out=AR.ap(B_PT + gh * 1024 + u * 512, 512),
                            in_=ps[:, ts(u, 512)],
                            func=AF.Exp), (rps,), (r_pts[gh * 2 + u],))

                def emit_ctx(tk, gh, a=a):
                    for u in range(2):
                        g = gh * 2 + u
                        top(lambda e, tk=tk, a=a, g=g, gh=gh, u=u: e.matmul(
                            PS_C[0:65, ts(g, 512)],
                            AR.ap(B_VT + tk * 130 + 65 * a, 65),
                            AR.ap(B_PT + gh * 1024 + u * 512, 512),
                            start=(tk == 0), stop=(tk == NT - 1)),
                            (r_vt, r_pts[gh * 2 + u]), (r_psc,))

                grans = [(tk, gh) for tk in range(NT) for gh in range(2)]
                for gi, (tk, gh) in enumerate(grans):
                    emit_st(tk, gh)
                    if not SKIP_CTX and gi >= 1:
                        emit_ctx(*grans[gi - 1])
                if not SKIP_CTX:
                    emit_ctx(*grans[-1])
                if SKIP_NORM or SKIP_CTX:
                    continue
                vop(lambda e: e.reciprocal(out=AR.ap(B_PT, 2048, 64, 65),
                    in_=PS_C[64:65, 0:2048]), (r_psc,), tuple(r_pts))
                if p == PTAP and a == 0:
                    t6 = mktap(f"brcp_{si}", (1, S))
                    if t6 is not None:
                        dma("gpq", t6, AR.ap(B_PT, 2048, 64, 65), reads=tuple(r_pts))
                for g in range(NG):
                    psb_ = (PS_A if g % 2 == 0 else PS_B)[0:64, 0:512]
                    rpsb = r_psa if g % 2 == 0 else r_psb
                    top(lambda e, g=g, psb_=psb_: e.matmul(
                        psb_, c_ones[64:65, 0:64], AR.ap(B_PT + g * 512, 512, 64, 65),
                        start=True, stop=True), (r_pts[g], r_const), (rpsb,))
                    sop(lambda e, psb_=psb_: e.activation(out=sm[0:64, 0:512],
                        in_=psb_, func=AF.Copy), (rpsb,), (r_sm,))
                    if a == 0:
                        vop(lambda e, p=p, g=g: e.tensor_mul(
                            AR.ap(O_CTX + p * S + g * 512, 512, 0, 64),
                            PS_C[0:64, ts(g, 512)], sm[0:64, 0:512]),
                            (r_psc, r_sm), (r_ctx,))
                    else:
                        vop(lambda e, g=g: e.tensor_mul(
                            AR.ap(B_SX, 512, 0, 64), PS_C[0:64, ts(g, 512)],
                            sm[0:64, 0:512]), (r_psc, r_sm), (r_sx,))
                        dma("gpq", AR.ap(O_CTX + p * S + g * 512, 512, 64, 128),
                            AR.ap(B_SX, 512, 0, 64), reads=(r_sx,), writes=(r_ctx,))
            if BARRIER:
                s.barrier()
        tap = mktap(f"ctx_{si}", (C, S))
        if tap is not None:
            for cc in range(NCH):
                dma("gpq", tap[ts(cc, 128), :], AR.ap(O_CTX + cc * S, S), reads=(r_ctx,))
        return r_ctx

    def phase_C(si, r_ctx):
        r_wo = AR.claim(O_XN, 8192, "WoT")
        r_xt = [AR.claim(O_XN + 8192 + i * C, C, f"cxt{i}") for i in range(2)]
        r_x1 = AR.claim(O_X1, 16384, "x1")
        for cc in range(NCH):
            dma("syncq", AR.ap(O_XN + cc * 1024, 1024), WoT_d[ts(cc, 128), :],
                pwrites=(r_wo,))
        for it in range(NT):
            xt = AR.ap(O_XN + 8192 + (it % 2) * C, C)
            rx = r_xt[it % 2]
            dma("syncq", xt, x0[si, ts(it, TT), :], writes=(rx,))
            pshalf = PS_C[:, (it % 2) * 1024:(it % 2) * 1024 + 1024]
            for og in range(2):
                for cc in range(NCH):
                    top(lambda e, it=it, og=og, cc=cc, pshalf=pshalf: e.matmul(
                        pshalf[:, ts(og, 512)],
                        AR.ap(O_CTX + cc * S + it * 128, 128),
                        AR.ap(O_XN + cc * 1024 + og * 512, 512),
                        start=(cc == 0), stop=(cc == NCH - 1)),
                        (r_ctx, r_wo), (r_psc,))
            vop(lambda e, it=it, xt=xt, pshalf=pshalf: e.tensor_add(
                AR.ap(O_X1 + it * 1024, 1024), pshalf, xt),
                (r_psc, rx), (r_x1,))
        tap = mktap(f"x1c_{si}", (S, C))
        if tap is not None:
            for it in range(NT):
                dma("gpq", tap[ts(it, TT), :], AR.ap(O_X1 + it * 1024, 1024), reads=(r_x1,))
        return r_x1

    def phase_D(si, r_x1):
        r_w1 = AR.claim(O_XN, 8192, "W1s")
        r_w2 = AR.claim(O_XN + 8192, 8192, "W2")
        r_xh = AR.claim(O_CTX, 4096, "xhat")
        r_h = AR.claim(O_CTX + 4096, 4096, "hsb")
        r_sl = AR.claim(O_CTX + 8192, 1024, "dslab")
        r_xt = [AR.claim(O_AUX + i * C, C, f"dxt{i}") for i in range(2)]
        for half in range(2):
            for el in range(4):
                ei = half * 4 + el
                for cc in range(NCH):
                    dma("syncq", AR.ap(O_XN + el * 2048 + cc * 256, 256),
                        W1_d[ei, ts(cc, 128), :], pwrites=(r_w1,))
                for cc2 in range(2):
                    dma("syncq", AR.ap(O_XN + 8192 + el * 2048 + cc2 * 1024, 1024),
                        W2_d[ei, ts(cc2, 128), :], pwrites=(r_w2,))
                for cc in range(NCH):
                    top(lambda e, el=el, cc=cc, ei=ei: e.matmul(
                        PS_B[0:1, 0:256], c_moe[:, 64 + ei * 8 + cc:64 + ei * 8 + cc + 1],
                        AR.ap(O_XN + el * 2048 + cc * 256, 256),
                        start=(cc == 0), stop=(cc == NCH - 1)), (r_w1, r_const), (r_psb,))
                for cc in range(NCH):
                    vop(lambda e, el=el, cc=cc, ei=ei: e.tensor_scalar(
                        out=AR.ap(O_XN + el * 2048 + cc * 256, 256),
                        in0=AR.ap(O_XN + el * 2048 + cc * 256, 256),
                        scalar1=c_moe[:, ei * 8 + cc:ei * 8 + cc + 1], scalar2=None,
                        op0=ALU.mult), (r_w1, r_const), (r_w1,))
                sop(lambda e: e.activation(out=sm[64:65, 0:256], in_=PS_B[0:1, 0:256],
                    func=AF.Copy), (r_psb,), (r_sm,))
                for hc in range(2):
                    top(lambda e, hc=hc: e.transpose(
                        PS_B[:, 256 + hc * 128:384 + hc * 128] if False
                        else PS_B[0:128, 256 + hc:257 + hc],
                        sm[64:65, hc * 128:(hc + 1) * 128], c_ident[64:65, 64:65]),
                        (r_sm, r_const), (r_psb,))
                    sop(lambda e, hc=hc, el=el: e.activation(
                        out=sm[:, 528 + el * 2 + hc:529 + el * 2 + hc],
                        in_=PS_B[0:128, 256 + hc:257 + hc], func=AF.Copy),
                        (r_psb,), (r_sm,))
                vop(lambda e, el=el, ei=ei: e.tensor_add(
                    sm[:, 512 + el * 2:514 + el * 2], sm[:, 528 + el * 2:530 + el * 2],
                    c_misc[:, 128 + ei * 2:130 + ei * 2]), (r_sm, r_const), (r_sm,))
            for g in range(NG):
                for tl in range(4):
                    it = g * 4 + tl
                    xt = AR.ap(O_AUX + (tl % 2) * C, C)
                    rx = r_xt[tl % 2]
                    dma("syncq", xt, x0[si, ts(it, TT), :], writes=(rx,))
                    sop(lambda e, xt=xt, it=it: e.activation(
                        out=AR.ap(O_CTX + 8192, 1024), in_=xt, func=AF.Identity,
                        scale=st_aff[:, 4 * it:4 * it + 1],
                        bias=st_aff[:, 4 * it + 1:4 * it + 2]),
                        (rx, r_aff), (r_sl,))
                    for cc in range(NCH):
                        pst = (PS_A if cc < 4 else PS_B)[:, ts(cc % 4, 128)]
                        rps = r_psa if cc < 4 else r_psb
                        top(lambda e, cc=cc, pst=pst: e.transpose(
                            pst, AR.ap(O_CTX + 8192 + cc * 128, 128), c_ident[:, :]),
                            (r_sl, r_const), (rps,))
                    sop(lambda e, tl=tl: e.activation(
                        out=AR.ap(O_CTX, 4096).rearrange("p (c t) -> p c t", c=NCH)
                        [:, 0:4, tl * 128:(tl + 1) * 128],
                        in_=PS_A[:, 0:512].rearrange("p (c t) -> p c t", c=4),
                        func=AF.Copy), (r_psa,), (r_xh,))
                    sop(lambda e, tl=tl: e.activation(
                        out=AR.ap(O_CTX, 4096).rearrange("p (c t) -> p c t", c=NCH)
                        [:, 4:8, tl * 128:(tl + 1) * 128],
                        in_=PS_B[:, 0:512].rearrange("p (c t) -> p c t", c=4),
                        func=AF.Copy), (r_psb,), (r_xh,))
                if half == 0 and g == 0:
                    t10 = mktap(f"dxh_{si}", (128, 4096))
                    if t10 is not None:
                        dma("gpq", t10, AR.ap(O_CTX, 4096), reads=(r_xh,))
                for el in range(4):
                    ei = half * 4 + el
                    for hc in range(2):
                        psh = PS_A[:, ts(hc, 512)]
                        for cc in range(NCH):
                            top(lambda e, el=el, hc=hc, cc=cc, psh=psh: e.matmul(
                                psh, AR.ap(O_XN + el * 2048 + cc * 256 + hc * 128, 128),
                                AR.ap(O_CTX + cc * 512, 512),
                                start=(cc == 0), stop=(cc == NCH - 1)),
                                (r_w1, r_xh), (r_psa,))
                        sop(lambda e, el=el, hc=hc, psh=psh: e.activation(
                            out=AR.ap(O_CTX + 4096 + el * 1024 + hc * 512, 512),
                            in_=psh, func=AF.Gelu,
                            bias=sm[:, 512 + el * 2 + hc:513 + el * 2 + hc]),
                            (r_psa, r_sm), (r_h,))
                        if half == 0 and el == 0 and g == 0:
                            t8 = mktap(f"dh{hc}_{si}", (128, 512))
                            if t8 is not None:
                                dma("gpq", t8, AR.ap(O_CTX + 4096 + hc * 512, 512), reads=(r_h,))
                            t9 = mktap(f"dbf{hc}_{si}", (128, 1))
                            if t9 is not None:
                                dma("gpq", t9, sm[:, 512 + hc:513 + hc], reads=(r_sm,))
                    dma("gpq", sm[64:65, 0:512], st_rows[32 + ei:33 + ei, ts(g, 512)],
                        reads=(r_rows,), writes=(r_sm,))
                    top(lambda e: e.matmul(PS_B[:, 512:1024], c_ones[64:65, 0:128],
                        sm[64:65, 0:512], start=True, stop=True), (r_sm, r_const), (r_psb,))
                    for hc in range(2):
                        vop(lambda e, el=el, hc=hc: e.tensor_mul(
                            AR.ap(O_CTX + 4096 + el * 1024 + hc * 512, 512),
                            AR.ap(O_CTX + 4096 + el * 1024 + hc * 512, 512),
                            PS_B[:, 512:1024]), (r_h, r_psb), (r_h,))
                    if half == 0 and el == 0 and g == 0:
                        t11 = mktap(f"dhm_{si}", (128, 512))
                        if t11 is not None:
                            dma("gpq", t11, AR.ap(O_CTX + 4096, 512), reads=(r_h,))
                    if half == 0 and el == 1 and g == 0:
                        t13 = mktap(f"dhm1_{si}", (128, 1024))
                        if t13 is not None:
                            dma("gpq", t13, AR.ap(O_CTX + 4096 + 1024, 1024), reads=(r_h,))
                        t14 = mktap(f"dw2e1_{si}", (128, 2048))
                        if t14 is not None:
                            dma("gpq", t14, AR.ap(O_XN + 8192 + 2048, 2048), reads=(r_w2,))
                for tl in range(4):
                    it = g * 4 + tl
                    psx = PS_C[:, (tl % 2) * 1024:(tl % 2) * 1024 + 1024]
                    for og in range(2):
                        for el in range(4):
                            for hc in range(2):
                                last = (half == 0 and el == 3 and hc == 1)
                                top(lambda e, el=el, hc=hc, og=og, tl=tl, psx=psx,
                                    last=last: e.matmul(
                                    psx[:, ts(og, 512)],
                                    AR.ap(O_CTX + 4096 + el * 1024 + hc * 512 + tl * 128, 128),
                                    AR.ap(O_XN + 8192 + el * 2048 + hc * 1024 + og * 512, 512),
                                    start=(el == 0 and hc == 0), stop=last),
                                    (r_h, r_w2), (r_psc,))
                        if half == 1:
                            top(lambda e, og=og, it=it, psx=psx: e.matmul(
                                psx[:, ts(og, 512)], st_rows[32:41, ts(it, TT)],
                                c_rows[32:41, og * 512:(og + 1) * 512],
                                start=False, stop=True), (r_rows, r_const), (r_psc,))
                    if half == 0 and it == 0:
                        t12 = mktap(f"dpsx_{si}", (128, 1024))
                        if t12 is not None:
                            sop(lambda e, psx=psx: e.activation(out=sm[:, 0:1024],
                                in_=psx, func=AF.Copy), (r_psc,), (r_sm,))
                            dma("gpq", t12, sm[:, 0:1024], reads=(r_sm,))
                    vop(lambda e, it=it, psx=psx: e.tensor_add(
                        AR.ap(O_X1 + it * 1024, 1024), AR.ap(O_X1 + it * 1024, 1024),
                        psx), (r_psc, r_x1), (r_x1,))
            if half == 0:
                taph = mktap(f"x1h_{si}", (S, C))
                if taph is not None:
                    for it2 in range(NT):
                        dma("gpq", taph[ts(it2, TT), :], AR.ap(O_X1 + it2 * 1024, 1024),
                            reads=(r_x1,))
        tap = mktap(f"x1d_{si}", (S, C))
        if tap is not None:
            for it in range(NT):
                dma("gpq", tap[ts(it, TT), :], AR.ap(O_X1 + it * 1024, 1024), reads=(r_x1,))
        return r_x1


    def phase_E(si, r_x1):
        r_xn1 = AR.claim(O_XN, 16384, "xn1")
        r_sl = AR.claim(O_CTX, 1024, "esl")
        cS = st_cols
        for it in range(NT):
            xt = AR.ap(O_X1 + it * 1024, 1024)
            sl = AR.ap(O_CTX, 1024)
            vop(lambda e, xt=xt: e.reduce_sum(cS[:, 0:1], xt, axis=AX.X), (r_x1,), (r_stats,))
            sop(lambda e, xt=xt, sl=sl: e.activation(out=sl, in_=xt, func=AF.Square,
                accum_out=cS[:, 1:2]), (r_x1,), (r_sl, r_stats))
            vop(lambda e: e.tensor_scalar(out=cS[:, 2:3], in0=cS[:, 0:1],
                scalar1=1.0 / C, scalar2=None, op0=ALU.mult), (r_stats,), (r_stats,))
            vop(lambda e: e.tensor_scalar(out=cS[:, 3:4], in0=cS[:, 1:2],
                scalar1=1.0 / C, scalar2=None, op0=ALU.mult), (r_stats,), (r_stats,))
            vop(lambda e: e.scalar_tensor_tensor(out=cS[:, 4:5], in0=cS[:, 2:3],
                scalar=cS[:, 2:3], in1=cS[:, 3:4], op0=ALU.mult, op1=ALU.subtract),
                (r_stats,), (r_stats,))
            vop(lambda e: e.scalar_tensor_tensor(out=cS[:, 16:17], in0=cS[:, 4:5],
                scalar=-1.0, in1=c_eps[:, 0:1], op0=ALU.mult, op1=ALU.add),
                (r_stats, r_const), (r_stats,))
            sop(lambda e: e.activation(out=cS[:, 5:6], in_=cS[:, 16:17], func=AF.Sqrt),
                (r_stats,), (r_stats,))
            vop(lambda e: e.reciprocal(out=cS[:, 6:7], in_=cS[:, 5:6]), (r_stats,), (r_stats,))
            vop(lambda e: e.scalar_tensor_tensor(out=cS[:, 7:8], in0=cS[:, 2:3],
                scalar=-1.0, in1=cS[:, 6:7], op0=ALU.mult, op1=ALU.mult),
                (r_stats,), (r_stats,))
            sop(lambda e, xt=xt, sl=sl: e.activation(out=sl, in_=xt, func=AF.Identity,
                scale=cS[:, 6:7], bias=cS[:, 7:8]), (r_x1, r_stats), (r_sl,))
            for cc in range(NCH):
                pst = (PS_A if cc < 4 else PS_B)[:, ts(cc % 4, 128)]
                rps = r_psa if cc < 4 else r_psb
                top(lambda e, sl=sl, cc=cc, pst=pst: e.transpose(
                    pst, sl[:, ts(cc, 128)], c_ident[:, :]), (r_sl, r_const), (rps,))
            sop(lambda e, it=it: e.activation(
                out=AR.ap(O_XN, 8 * S).rearrange("p (c t) -> p c t", c=NCH)[:, 0:4, ts(it, TT)],
                in_=PS_A[:, 0:512].rearrange("p (c t) -> p c t", c=4), func=AF.Copy),
                (r_psa,), (r_xn1,))
            sop(lambda e, it=it: e.activation(
                out=AR.ap(O_XN, 8 * S).rearrange("p (c t) -> p c t", c=NCH)[:, 4:8, ts(it, TT)],
                in_=PS_B[:, 0:512].rearrange("p (c t) -> p c t", c=4), func=AF.Copy),
                (r_psb,), (r_xn1,))
        tap = mktap(f"xn1_{si}", (C, S))
        if tap is not None:
            for cc in range(NCH):
                dma("gpq", tap[ts(cc, 128), :], fmap(O_XN, cc, 0, S), reads=(r_xn1,))
        return r_xn1

    def phase_F(si, r_xn1, r_x1):
        r_w = AR.claim(O_CTX, 16384, "w1attn")
        r_blk = AR.claim(O_AUX, 1024, "fblk")
        r_sq = AR.claim(O_AUX + 1024, 1024, "fsq")
        cS = st_cols
        for oc in range(NCH):
            for cc in range(NCH):
                dma("syncq", AR.ap(O_AUX + cc * 128, 128),
                    qkvT_d[ts(cc, 128), ts(oc, 128)], pwrites=(r_blk,))
            for g in range(NG):
                psw = (PS_A if g % 2 == 0 else PS_B)[:, 0:512]
                rps = r_psa if g % 2 == 0 else r_psb
                for cc in range(NCH):
                    top(lambda e, cc=cc, g=g, psw=psw: e.matmul(
                        psw, AR.ap(O_AUX + cc * 128, 128), fmap(O_XN, cc, g * TG, (g + 1) * TG),
                        start=(cc == 0), stop=(cc == NCH - 1)), (r_blk, r_xn1), (rps,))
                sop(lambda e, oc=oc, g=g, psw=psw: e.activation(
                    out=fmap(O_CTX, oc, g * TG, (g + 1) * TG), in_=psw, func=AF.Copy),
                    (rps,), (r_w,))
                sop(lambda e, g=g, psw=psw: e.activation(
                    out=AR.ap(O_AUX + 1024, 512), in_=psw, func=AF.Square,
                    accum_out=cS[:, 20 + g:21 + g]), (rps,), (r_sq, r_stats))
            vop(lambda e, oc=oc: e.reduce_sum(cS[:, 32 + oc:33 + oc], cS[:, 20:24],
                axis=AX.X), (r_stats,), (r_stats,))
        vop(lambda e: e.tensor_scalar(out=cS[:, 40:48], in0=cS[:, 32:40],
            scalar1=1e-24, scalar2=None, op0=ALU.max), (r_stats,), (r_stats,))
        vop(lambda e: e.reciprocal(out=cS[:, 48:56], in_=cS[:, 40:48]), (r_stats,), (r_stats,))
        vop(lambda e: e.memset(sm[:, 0:128], 0.0), (), (r_sm,))
        for oc in range(NCH):
            vop(lambda e, oc=oc: e.tensor_copy(sm[0:64, oc * 16 + 2 * oc:oc * 16 + 2 * oc + 1],
                cS[0:64, 48 + oc:49 + oc]), (r_stats,), (r_sm,))
            vop(lambda e, oc=oc: e.tensor_copy(sm[64:128, oc * 16 + 2 * oc + 1:oc * 16 + 2 * oc + 2],
                cS[64:128, 48 + oc:49 + oc]), (r_stats,), (r_sm,))
        for g in range(NG):
            for oc in range(NCH):
                sop(lambda e, oc=oc, g=g: e.activation(out=AR.ap(O_AUX + 1024, 512),
                    in_=fmap(O_CTX, oc, g * TG, (g + 1) * TG), func=AF.Square),
                    (r_w,), (r_sq,))
                top(lambda e, oc=oc, g=g: e.matmul(
                    PS_C[0:16, ts(g, TG)], sm[:, oc * 16:oc * 16 + 16], AR.ap(O_AUX + 1024, 512),
                    start=(oc == 0), stop=(oc == NCH - 1)), (r_sq, r_sm), (r_psc,))
        sop(lambda e: e.activation(out=st_rows[0:16, :], in_=PS_C[0:16, :], func=AF.Identity,
            scale=c_temp[:, 0:1]), (r_psc, r_const), (r_rows,))
        for it in range(NT):
            pst = (PS_A if it % 2 == 0 else PS_B)[:, 0:16]
            rps = r_psa if it % 2 == 0 else r_psb
            top(lambda e, it=it, pst=pst: e.transpose(
                pst, st_rows[0:16, ts(it, TT)], c_ident[0:16, 0:16]), (r_rows, r_const), (rps,))
            sop(lambda e, pst=pst: e.activation(out=sm[:, 128:144], in_=pst, func=AF.Copy),
                (rps,), (r_sm,))
            vop(lambda e: e.reduce_max(cS[:, 0:1], sm[:, 128:144], axis=AX.X, negate=True),
                (r_sm,), (r_stats,))
            sop(lambda e: e.activation(out=sm[:, 144:160], in_=sm[:, 128:144], func=AF.Exp,
                bias=cS[:, 0:1], accum_out=cS[:, 1:2]), (r_sm, r_stats), (r_sm, r_stats))
            vop(lambda e: e.reciprocal(out=cS[:, 2:3], in_=cS[:, 1:2]), (r_stats,), (r_stats,))
            vop(lambda e, it=it: e.tensor_scalar(out=sm[:, 256 + it * 16:272 + it * 16],
                in0=sm[:, 144:160], scalar1=cS[:, 2:3], scalar2=None, op0=ALU.mult),
                (r_sm, r_stats), (r_sm,))
            pst2 = (PS_A if it % 2 == 0 else PS_B)[:, 128:256]
            top(lambda e, it=it, pst2=pst2: e.transpose(
                pst2[0:16, :], sm[:, 256 + it * 16:272 + it * 16], c_ident[:, :]),
                (r_sm, r_const), (rps,))
            sop(lambda e, it=it, pst2=pst2: e.activation(out=st_rows[0:16, ts(it, TT)],
                in_=pst2[0:16, :], func=AF.Copy), (rps,), (r_rows,))
        vop(lambda e: e.reduce_sum(cS[0:16, 8:9], st_rows[0:16, :], axis=AX.X),
            (r_rows,), (r_stats,))
        vop(lambda e: e.scalar_tensor_tensor(out=cS[0:16, 9:10], in0=cS[0:16, 8:9],
            scalar=1.0, in1=c_eps8[0:16, 0:1], op0=ALU.bypass, op1=ALU.add),
            (r_stats, r_const), (r_stats,))
        vop(lambda e: e.reciprocal(out=cS[0:16, 10:11], in_=cS[0:16, 9:10]),
            (r_stats,), (r_stats,))
        for it in range(NT):
            sop(lambda e, it=it: e.activation(
                out=AR.ap(O_AUX + 1024, 1024), in_=AR.ap(O_CTX, 8 * S).rearrange(
                    "p (c t) -> p c t", c=NCH)[:, :, ts(it, TT)], func=AF.Square),
                (r_w,), (r_sq,))
            for cc in range(NCH):
                pst = (PS_A if cc < 4 else PS_B)[:, ts(cc % 4, 128)]
                rps = r_psa if cc < 4 else r_psb
                top(lambda e, cc=cc, pst=pst: e.transpose(
                    pst, AR.ap(O_AUX + 1024 + cc * 128, 128), c_ident[:, :]),
                    (r_sq, r_const), (rps,))
            sop(lambda e: e.activation(out=AR.ap(O_AUX, 512), in_=PS_A[:, 0:512],
                func=AF.Copy), (r_psa,), (r_blk,))
            sop(lambda e: e.activation(out=AR.ap(O_AUX + 512, 512), in_=PS_B[:, 0:512],
                func=AF.Copy), (r_psb,), (r_blk,))
            for og in range(2):
                top(lambda e, it=it, og=og: e.matmul(
                    PS_C[0:16, og * 512:(og + 1) * 512],
                    sm[:, 256 + it * 16:272 + it * 16], AR.ap(O_AUX + og * 512, 512),
                    start=(it == 0), stop=(it == NT - 1)), (r_sm, r_blk), (r_psc,))
        sop(lambda e: e.activation(out=sm[0:16, 0:1024], in_=PS_C[0:16, 0:1024],
            func=AF.Identity, scale=cS[0:16, 10:11]), (r_psc, r_stats), (r_sm,))
        vop(lambda e: e.tensor_scalar(out=sm[0:16, 0:1024], in0=sm[0:16, 0:1024],
            scalar1=1.0, scalar2=None, op0=ALU.add), (r_sm,), (r_sm,))
        vop(lambda e: e.reciprocal(out=AR.ap(O_AUX + 1024, 1024, 0, 16), in_=sm[0:16, 0:1024]),
            (r_sm,), (r_sq,))
        vop(lambda e: e.tensor_mul(sm[0:16, 0:1024], c_rows[0:16, :],
            AR.ap(O_AUX + 1024, 1024, 0, 16)), (r_const, r_sq), (r_sm,))
        for oc in range(NCH):
            for g in range(NG):
                psx = (PS_A if g % 2 == 0 else PS_B)[:, 0:512]
                rps = r_psa if g % 2 == 0 else r_psb
                top(lambda e, oc=oc, g=g, psx=psx: e.matmul(
                    psx, sm[0:16, oc * 128:(oc + 1) * 128], st_rows[0:16, ts(g, TG)],
                    start=True, stop=True), (r_sm, r_rows), (rps,))
                vop(lambda e, oc=oc, g=g, psx=psx: e.tensor_mul(
                    fmap(O_CTX, oc, g * TG, (g + 1) * TG),
                    fmap(O_CTX, oc, g * TG, (g + 1) * TG), psx), (r_w, rps), (r_w,))
        r_ow = AR.claim(O_AUX, 2048, "fow")
        for oq in range(4):
            for cc in range(NCH):
                dma("syncq", AR.ap(O_AUX + cc * 256, 256),
                    outwT_d[ts(cc, 128), oq * 256:(oq + 1) * 256], pwrites=(r_ow,))
            for it in range(NT):
                psx = (PS_A if it % 2 == 0 else PS_B)[:, 0:256]
                rps = r_psa if it % 2 == 0 else r_psb
                for cc in range(NCH):
                    top(lambda e, cc=cc, it=it, oq=oq, psx=psx: e.matmul(
                        psx, AR.ap(O_CTX + cc * S + it * 128, 128),
                        AR.ap(O_AUX + cc * 256, 256), start=(cc == 0), stop=False),
                        (r_w, r_ow), (rps,))
                top(lambda e, oq=oq, it=it, psx=psx: e.matmul(
                    psx, c_ones[64:65, 0:128], c_rows[64:65, oq * 256:(oq + 1) * 256],
                    start=False, stop=True), (r_const,), (rps,))
                vop(lambda e, it=it, oq=oq, psx=psx: e.tensor_add(
                    sm[:, 0:256], psx,
                    AR.ap(O_X1 + it * 1024 + oq * 256, 256)), (rps, r_x1), (r_sm,))
                dma("gpq", scr_a1[ts(it, TT), oq * 256:(oq + 1) * 256],
                    sm[:, 0:256], reads=(r_sm,), writes=(r_scr,))
        tap = mktap(f"a1_{si}", (S, C))
        if tap is not None:
            dma("gpq", tap, scr_a1, reads=(r_scr,))

    def phase_G(si, r_xn1):
        r_blkg = AR.claim(O_AUX, 1024, "gblk")
        r_sqg = AR.claim(O_AUX + 1024, 1024, "gsq")
        r_ka = AR.claim(O_X1, 16384, "k_act")
        r_wk = AR.claim(O_CTX, 8192, "WkT1")
        r_xg = AR.claim(O_CTX + 8192, 4096, "xk_g")
        r_t1 = AR.claim(O_CTX + 12288, 4096, "t1sl")
        for cc in range(NCH):
            dma("syncq", AR.ap(O_CTX + cc * 1024, 1024), WkT1_d[ts(cc, 128), :],
                pwrites=(r_wk,))

        def build_shift(g, mucols, onemcols, r_dst, dst_off):
            for cc in range(NCH):
                if g == 0:
                    vop(lambda e, cc=cc: e.memset(AR.ap(O_CTX + 12288 + cc * 512, 1), 0.0),
                        (), (r_t1,))
                    sop(lambda e, cc=cc, onemcols=onemcols: e.activation(
                        out=AR.ap(O_CTX + 12288 + cc * 512 + 1, 511),
                        in_=AR.ap(O_XN + cc * S, 511), func=AF.Identity,
                        scale=c_misc[:, onemcols + cc:onemcols + cc + 1]),
                        (r_xn1, r_const), (r_t1,))
                else:
                    sop(lambda e, cc=cc, g=g, onemcols=onemcols: e.activation(
                        out=AR.ap(O_CTX + 12288 + cc * 512, 512),
                        in_=AR.ap(O_XN + cc * S + g * 512 - 1, 512), func=AF.Identity,
                        scale=c_misc[:, onemcols + cc:onemcols + cc + 1]),
                        (r_xn1, r_const), (r_t1,))
                vop(lambda e, cc=cc, g=g, mucols=mucols, dst_off=dst_off: e.scalar_tensor_tensor(
                    out=AR.ap(dst_off + cc * 512, 512),
                    in0=AR.ap(O_XN + cc * S + g * 512, 512),
                    scalar=c_misc[:, mucols + cc:mucols + cc + 1],
                    in1=AR.ap(O_CTX + 12288 + cc * 512, 512),
                    op0=ALU.mult, op1=ALU.add), (r_xn1, r_const, r_t1), (r_dst,))

        for g in range(NG):
            build_shift(g, 152, 168, r_xg, O_CTX + 8192)
            for oc in range(NCH):
                psw = (PS_A if oc % 2 == 0 else PS_B)[:, 0:512]
                rps = r_psa if oc % 2 == 0 else r_psb
                for cc in range(NCH):
                    top(lambda e, oc=oc, cc=cc, psw=psw: e.matmul(
                        psw, AR.ap(O_CTX + cc * 1024 + oc * 128, 128),
                        AR.ap(O_CTX + 8192 + cc * 512, 512),
                        start=(cc == 0), stop=(cc == NCH - 1)), (r_wk, r_xg), (rps,))
                sop(lambda e, oc=oc, g=g, psw=psw: e.activation(
                    out=AR.ap(O_AUX, 512), in_=psw, func=AF.Relu), (rps,), (r_blkg,))
                sop(lambda e, oc=oc, g=g: e.activation(
                    out=fmap(O_X1, oc, g * TG, (g + 1) * TG), in_=AR.ap(O_AUX, 512),
                    func=AF.Square), (r_blkg,), (r_ka,))
        r_wr = AR.claim(O_CTX, 8192, "WrT")
        for cc in range(NCH):
            dma("syncq", AR.ap(O_CTX + cc * 1024, 1024), WrT_d[ts(cc, 128), :],
                pwrites=(r_wr,))
        for g in range(NG):
            build_shift(g, 144, 160, r_xg, O_CTX + 8192)
            for tl in range(4):
                it = g * 4 + tl
                psx = PS_C[:, (tl % 2) * 1024:(tl % 2) * 1024 + 1024]
                for og in range(2):
                    for cc in range(NCH):
                        top(lambda e, cc=cc, og=og, tl=tl, psx=psx: e.matmul(
                            psx[:, ts(og, 512)],
                            AR.ap(O_CTX + 8192 + cc * 512 + tl * 128, 128),
                            AR.ap(O_CTX + cc * 1024 + og * 512, 512),
                            start=(cc == 0), stop=(cc == NCH - 1)), (r_wr, r_xg), (r_psc,))
                sop(lambda e, psx=psx: e.activation(out=AR.ap(O_AUX + 1024, 1024),
                    in_=psx, func=AF.Sigmoid), (r_psc,), (r_sqg,))
                dma("gpq", scr_r[ts(it, TT), :], AR.ap(O_AUX + 1024, 1024),
                    reads=(r_sqg,), writes=(r_scr,))
        r_wv = AR.claim(O_CTX, 8192, "WvT1")
        r_pf = AR.claim(O_CTX + 8192, 4096, "prefetch")
        r_x2 = AR.claim(O_CTX + 12288, 2048, "x2sl")
        for cc in range(NCH):
            dma("syncq", AR.ap(O_CTX + cc * 1024, 1024), WvT1_d[ts(cc, 128), :],
                pwrites=(r_wv,))
        for it in range(NT):
            sl = (it % 2) * 1024
            dma("syncq", AR.ap(O_CTX + 8192 + sl, 1024), scr_r[ts(it, TT), :],
                reads=(r_scr,), pwrites=(r_pf,))
            dma("syncq", AR.ap(O_CTX + 10240 + sl, 1024), scr_a1[ts(it, TT), :],
                reads=(r_scr,), pwrites=(r_pf,))
            psx = PS_C[:, sl:sl + 1024]
            for og in range(2):
                for cc in range(NCH):
                    top(lambda e, cc=cc, og=og, it=it, psx=psx: e.matmul(
                        psx[:, ts(og, 512)],
                        AR.ap(O_X1 + cc * S + it * 128, 128),
                        AR.ap(O_CTX + cc * 1024 + og * 512, 512),
                        start=(cc == 0), stop=(cc == NCH - 1)), (r_wv, r_ka), (r_psc,))
            vop(lambda e, sl=sl, psx=psx: e.tensor_mul(
                AR.ap(O_CTX + 12288 + sl, 1024), psx, AR.ap(O_CTX + 8192 + sl, 1024)),
                (r_psc, r_pf), (r_x2,))
            vop(lambda e, sl=sl: e.tensor_add(
                AR.ap(O_CTX + 12288 + sl, 1024), AR.ap(O_CTX + 12288 + sl, 1024),
                AR.ap(O_CTX + 10240 + sl, 1024)), (r_x2, r_pf), (r_x2,))
            dma("gpq", y[si, ts(it, TT), :], AR.ap(O_CTX + 12288 + sl, 1024),
                reads=(r_x2,), writes=(r_y,))

    for si in range(NSEQ):
        r_xn = phase_A(si)
        if 'B' in PHASES:
            r_ctx = phase_B(si, r_xn)
        if 'C' in PHASES:
            r_x1 = phase_C(si, r_ctx)
        if 'D' in PHASES:
            r_x1 = phase_D(si, r_x1)
        if 'E' in PHASES:
            r_xn1 = phase_E(si, r_x1)
        if 'F' in PHASES:
            phase_F(si, r_xn1, r_x1)
        if 'G' in PHASES:
            phase_G(si, r_xn1)
        if NSEQ_DEV == 1:
            break

    s.fence("gpq", [r_y.w] if r_y.w else [])
    s.fence("syncq", [r_y.w] if r_y.w else [])

    with ctx:
        with contextlib.ExitStack() as semctx:
            sems = {}
            for e in ENGS:
                if e in DMA_Q:
                    for i in range(NSEM_POOL):
                        sems[f"{e}{i}"] = semctx.enter_context(nc.semaphore(f"sem_{e}{i}"))
                else:
                    sems[e] = semctx.enter_context(nc.semaphore("sem_" + e))
            with nc.Block() as block:
                run = s.finalize(sems)
                block.tensor(lambda t: run("tensor", t))
                block.vector(lambda v: run("vector", v))
                block.scalar(lambda sc: run("scalar", sc))
                block.sync(lambda sy: run("syncq", sy))
                block.gpsimd(lambda g: run("gpq", g))
    return nc, taps


def _host_inputs(kw):
    d = {}
    def _packT(W):
        return np.ascontiguousarray(
            W.T.reshape(8, 128, 8, 128).transpose(1, 2, 0, 3).reshape(128, 8192))
    d["WqT"] = _packT(kw["L0_attn_Wq"])
    d["WkT"] = _packT(kw["L0_attn_Wk"])
    d["WvT"] = _packT(kw["L0_attn_Wv"])
    d["WoT"] = np.ascontiguousarray(kw["L0_attn_Wo"].T)
    bq, bk, bv = kw["L0_attn_bq"], kw["L0_attn_bk"], kw["L0_attn_bv"]
    d["b_qkv"] = np.ascontiguousarray(
        np.stack([bq, bk, bv]).reshape(3, 8, 128).transpose(2, 0, 1).reshape(128, 24))
    sl = np.asarray(kw["L0_attn_slopes"], np.float32)
    sd = np.zeros((128, 8), np.float32)
    sd1 = np.zeros((128, 8), np.float32)
    sw1 = np.zeros((128, 8), np.float32)
    sd[64, :] = sl[0::2]; sd[65, :] = sl[1::2]
    sd1[64, :] = sl[0::2]; sd1[65, :] = 1.0
    sw1[64, :] = sl[1::2]; sw1[65, :] = 1.0
    d["slopes_diag"] = sd; d["slopes_d1"] = sd1; d["slopes_s1"] = sw1
    d["WgT"] = np.ascontiguousarray(np.asarray(kw["L0_ffn_Wg"]).T.reshape(8, 128, 8).transpose(1, 0, 2).reshape(128, 64))
    d["W1"] = np.ascontiguousarray(kw["L0_ffn_W1"])
    d["W2"] = np.ascontiguousarray(kw["L0_ffn_W2"])
    d["lnsc_c"] = np.ascontiguousarray(
        np.asarray(kw["L0_ffn_ln_scale"]).reshape(8, 8, 128).transpose(2, 0, 1).reshape(128, 64))
    d["lnb_c"] = np.ascontiguousarray(
        np.asarray(kw["L0_ffn_ln_bias"]).reshape(8, 8, 128).transpose(2, 0, 1).reshape(128, 64))
    d["b1c"] = np.ascontiguousarray(
        np.asarray(kw["L0_ffn_b1"]).reshape(8, 2, 128).transpose(2, 0, 1).reshape(128, 16))
    d["biasmm0"] = np.ascontiguousarray(
        np.concatenate([kw["L0_ffn_b2"], np.asarray(kw["L0_attn_bo"])[None]], axis=0))
    d["qkvT"] = np.ascontiguousarray(kw["L1_attn_qkv_w"].T)
    d["outwT"] = np.ascontiguousarray(kw["L1_attn_out_w"].T)
    d["biasmm1"] = np.ascontiguousarray(np.asarray(kw["L1_attn_out_b"])[None])
    d["temp16"] = np.ascontiguousarray(kw["L1_attn_temp"])
    d["WrT"] = np.ascontiguousarray(kw["L1_ffn_W_r"].T)
    d["WkT1"] = np.ascontiguousarray(kw["L1_ffn_W_k"].T)
    d["WvT1"] = np.ascontiguousarray(kw["L1_ffn_W_v"].T)
    d["mur_c"] = np.ascontiguousarray(np.asarray(kw["L1_ffn_mu_r"]).reshape(8, 128).T)
    d["muk_c"] = np.ascontiguousarray(np.asarray(kw["L1_ffn_mu_k"]).reshape(8, 128).T)
    d["ident"] = np.eye(128, dtype=np.float32)
    aug2 = np.zeros((128, S), np.float32)
    aug2[64, :] = np.arange(S, dtype=np.float32)
    aug2[65, :] = 1.0
    aug2[66, :] = 1.0
    d["aug2"] = aug2
    d["onesq"] = np.ones((128, 128), np.float32)
    e2v = np.zeros((128, 2), np.float32); e2v[0:64, 0] = 1.0; e2v[64:128, 1] = 1.0
    d["e2"] = e2v
    mb_ = np.zeros((16, C), np.float32)
    for h in range(16):
        mb_[h, h * 64:(h + 1) * 64] = -1.0
    d["maskblk"] = mb_
    return {k: np.ascontiguousarray(np.asarray(v, dtype=np.float32)) for k, v in d.items()}


def kernel(**kw):
    shared = _host_inputs(kw)
    x = np.ascontiguousarray(np.asarray(kw["L0_attn_x"], dtype=np.float32))
    nc, taps = build_program(DEBUG_TAPS)
    in_maps = []
    for core in range(8):
        m = dict(shared)
        m["x0"] = np.ascontiguousarray(x[2 * core:2 * core + 2])
        in_maps.append(m)
    res = run_bass_kernel_spmd(nc, in_maps, core_ids=list(range(8)))
    out = np.concatenate([r["y"] for r in res.results], axis=0)
    return out.reshape(16, S, C)
